# version 38
# baseline (speedup 1.0000x reference)
import contextlib
import numpy as np
import ml_dtypes
import concourse.bass as bass
import concourse.mybir as mybir
from concourse.alu_op_type import AluOpType as OP
from concourse.bass_utils import run_bass_kernel_spmd

F32 = mybir.dt.float32
BF16 = mybir.dt.bfloat16
AF = mybir.ActivationFunctionType
AX = mybir.AxisListType
bfdt = ml_dtypes.bfloat16

T, C = 4096, 1024
NT, NQC = 32, 8
NH, NKV, HD = 16, 4, 64
QKV, KVD = 1536, 256
I = 2816
NIC = 6
IC_W = [512] * 5 + [256]
IC_OFF = [0, 512, 1024, 1536, 2048, 2560]
NCORES = 8
ENGINES = ["sync", "gpsimd", "tensor", "vector", "scalar"]


class Prog:
    def __init__(self, nc, es):
        self.nc = nc
        self.steps = []
        self.sems = {}
        self.cnt = {}
        for e in ENGINES + ["dsync", "dgp"]:
            self.sems[e] = es.enter_context(nc.semaphore("sem_" + e))
            self.cnt[e] = 0
        self.last_plain = {"sync": None, "gpsimd": None}
        self.last_tp = None

    def step(self, eng, deps, fn):
        self.cnt[eng] += 1
        tok = (eng, self.cnt[eng])
        self.steps.append((eng, [d for d in deps if d is not None], fn, "c", tok))
        return tok

    def dma(self, eng, deps, out, in_, transpose=False):
        deps = list(deps)
        if transpose:
            deps += [t for t in (self.last_plain["sync"], self.last_plain["gpsimd"]) if t]
        elif self.last_tp is not None:
            deps.append(self.last_tp)
        d = {"sync": "dsync", "gpsimd": "dgp"}[eng]
        self.cnt[d] += 16
        dval = self.cnt[d]

        def fn(eh, out=out, in_=in_, transpose=transpose):
            if transpose:
                return eh.dma_start_transpose(out=out, in_=in_)
            return eh.dma_start(out=out, in_=in_)
        self.steps.append((eng, [x for x in deps if x is not None], fn, "d", (d, dval)))
        self.cnt[eng] += 1
        tok = (eng, self.cnt[eng])
        self.steps.append((eng, [(d, dval)], None, "k", tok))
        if transpose:
            self.last_tp = tok
        else:
            self.last_plain[eng] = tok
        return tok

    def dma_batch(self, eng, deps, xfers, transpose=False):
        deps = list(deps)
        if transpose:
            deps += [t for t in (self.last_plain["sync"], self.last_plain["gpsimd"]) if t]
        elif self.last_tp is not None:
            deps.append(self.last_tp)
        d = {"sync": "dsync", "gpsimd": "dgp"}[eng]
        dval = None
        for j, (out, in_) in enumerate(xfers):
            self.cnt[d] += 16
            dval = self.cnt[d]

            def fn(eh, out=out, in_=in_, transpose=transpose):
                if transpose:
                    return eh.dma_start_transpose(out=out, in_=in_)
                return eh.dma_start(out=out, in_=in_)
            self.steps.append((eng, [x for x in (deps if j == 0 else []) if x is not None],
                               fn, "d", (d, dval)))
        self.cnt[eng] += 1
        tok = (eng, self.cnt[eng])
        self.steps.append((eng, [(d, dval)], None, "k", tok))
        if transpose:
            self.last_tp = tok
        else:
            self.last_plain[eng] = tok
        return tok

    def emit(self, block):
        for eng in ENGINES:
            mysteps = [s for s in self.steps if s[0] == eng]
            if not mysteps:
                continue

            def make(mysteps=mysteps, eng=eng):
                def stream(eh):
                    waited = {}
                    for _, deps, fn, kind, tok in mysteps:
                        for (sem, val) in deps:
                            if sem == eng:
                                continue
                            if waited.get(sem, -1) >= val:
                                continue
                            eh.wait_ge(self.sems[sem], val)
                            waited[sem] = val
                        if kind == "k":
                            eh.sem_inc(self.sems[eng], 1)
                        elif kind == "c":
                            fn(eh).then_inc(self.sems[eng], 1)
                        else:
                            fn(eh).then_inc(self.sems[tok[0]], 16)
                    if eng == "sync":
                        if self.cnt["dsync"]:
                            eh.wait_ge(self.sems["dsync"], self.cnt["dsync"])
                        if self.cnt["dgp"]:
                            eh.wait_ge(self.sems["dgp"], self.cnt["dgp"])
                        if self.cnt["vector"]:
                            eh.wait_ge(self.sems["vector"], self.cnt["vector"])
                return stream
            getattr(block, eng)(make())


def bc3(ap2d, nblk, rep):
    return ap2d.unsqueeze(2).broadcast_to([ap2d.shape[0], nblk, rep])


def r16(ap2d):
    return ap2d.rearrange("p (a s) -> p a s", s=16)


OFF_B = 90112
OFF_C = OFF_B + 53248
ARENA_BYTES = OFF_C + 61440


def build_program():
    nc = bass.Bass("TRN2", target_bir_lowering=False)

    x_d = nc.dram_tensor("x", [T, C], F32, kind="ExternalInput")
    wattn_d = nc.dram_tensor("w_attn", [C, QKV], F32, kind="ExternalInput")
    wproj_d = nc.dram_tensor("w_proj", [C, C], F32, kind="ExternalInput")
    wgate_d = nc.dram_tensor("w_gate", [I, C], F32, kind="ExternalInput")
    wup_d = nc.dram_tensor("w_up", [I, C], F32, kind="ExternalInput")
    wdown_d = nc.dram_tensor("w_down", [C, I], F32, kind="ExternalInput")
    cos_d = nc.dram_tensor("cos2", [T, HD], F32, kind="ExternalInput")
    sin_d = nc.dram_tensor("sin2", [T, HD], F32, kind="ExternalInput")
    tri_d = nc.dram_tensor("tri", [128, 128], BF16, kind="ExternalInput")
    sel_d = nc.dram_tensor("sel", [128, 128], BF16, kind="ExternalInput")
    idn_d = nc.dram_tensor("idn", [128, 128], BF16, kind="ExternalInput")
    y_d = nc.dram_tensor("y", [T, C], F32, kind="ExternalOutput")
    xn_d = nc.dram_tensor("xn_s", [T, C], BF16)
    xq_d = nc.dram_tensor("xq_s", [T, C], BF16)
    q_d = nc.dram_tensor("q_s", [T, C], BF16)
    k_d = nc.dram_tensor("k_s", [T, 2 * KVD], BF16)
    wgq_d = nc.dram_tensor("wgq_s", [I, C], BF16)
    wuq_d = nc.dram_tensor("wuq_s", [I, C], BF16)
    wdq_d = nc.dram_tensor("wdq_s", [C, I], BF16)
    yt_d = nc.dram_tensor("yt_s", [C, T], BF16)
    resid_d = nc.dram_tensor("resid_s", [T, C], F32)
    l_d = nc.dram_tensor("l_s", [4, 512], F32)
    linv_d = nc.dram_tensor("linv_s", [4, 512], F32)

    es = contextlib.ExitStack()
    P = Prog(nc, es)
    sb = lambda n, sh, dt: es.enter_context(nc.sbuf_tensor(n, sh, dt))[:]
    arena_h = es.enter_context(nc.sbuf_tensor("arena", [128, ARENA_BYTES // 2], BF16))

    def carve(off_b, nelem, dt, parts=128):
        assert off_b % 64 == 0
        nb = nelem * (4 if dt == F32 else 2)
        v = arena_h[0:parts, off_b // 2:(off_b + nb) // 2]
        if dt != BF16:
            v = v.bitcast(dt)
        return v

    psA = es.enter_context(nc.psum_tensor("psA", [128, 2048], F32))
    psB = es.enter_context(nc.psum_tensor("psB", [128, 2048], F32))

    tri = sb("tri_s", [128, 128], BF16)
    sel = sb("sel_s", [128, 128], BF16)
    idn = sb("idn_s", [128, 128], BF16)
    L128 = sb("L128_s", [128, 512], BF16)

    t_tri = P.dma("sync", [], tri, tri_d[:, :])
    t_sel = P.dma("sync", [], sel, sel_d[:, :])
    t_idn = P.dma("sync", [], idn, idn_d[:, :])
    t_L0 = P.step("gpsimd", [], lambda g: g.memset(L128, 0.0))

    xt = [carve(4096 * b, C, F32) for b in range(3)]
    xn = [carve(12288 + 4096 * b, C, F32) for b in range(4)]
    qt_ = [carve(28672 + 4096 * b, C, F32) for b in range(2)]
    qu_ = [carve(36864 + 4096 * b, C, F32) for b in range(2)]
    qU4 = [carve(45056 + 4096 * b, C, F32) for b in range(2)]
    qP = [carve(53248 + 2048 * b, C, BF16) for b in range(2)]
    qQ = [carve(57344 + 2048 * b, C, BF16) for b in range(2)]
    qR1 = [carve(61440 + 2048 * b, C, BF16) for b in range(2)]
    qR2 = [carve(65536 + 2048 * b, C, BF16) for b in range(2)]
    qXQ = [carve(69632 + 2048 * b, C, BF16) for b in range(2)]
    xnTg = [carve(0, 8 * 512, BF16), carve(8192, 8 * 512, BF16)]
    wa = carve(16384, 8 * QKV, BF16)
    v65 = carve(40960, NT * 260, BF16)
    wp = carve(0, 8 * C, BF16)
    xb3 = [carve(16384 + 4096 * b, C, F32) for b in range(3)]
    rs3 = [carve(28672 + 4096 * b, C, F32) for b in range(3)]
    wgT = carve(0, 8 * I, BF16)
    wuT = carve(45056, 8 * I, BF16)
    wtb2 = [carve(57600 + 4096 * b, C, F32) for b in range(3)]
    qt2 = [carve(69888 + 4096 * b, C, F32) for b in range(2)]
    qu2 = [carve(78080 + 4096 * b, C, F32) for b in range(2)]

    B = OFF_B
    cosb = carve(B + 0, NT * HD, F32)
    sinb = carve(B + 8192, NT * HD, F32)
    m1 = carve(B + 16384, C + KVD, F32)
    m2 = carve(B + 21504, C + KVD, F32)
    rot = [carve(B + 26624, C + KVD, F32), carve(B + 31744, C + KVD, F32)]
    sqq = m1
    qn = [carve(B + 36864, C, BF16), carve(B + 38912, C, BF16)]
    kn = [carve(B + 40960, KVD, BF16), carve(B + 41472, KVD, BF16)]
    qTg = [carve(B + 0, 2 * T, BF16), carve(B + 16384, 2 * T, BF16)]
    kTd = carve(B + 32768, T, BF16)
    PT = [carve(B + 40960, 1024, BF16), carve(B + 43008, 1024, BF16), carve(B + 45056, 1024, BF16)]
    ybuf = carve(B + 51200, 512, BF16)
    wdT = carve(B + 0, 22 * C, BF16)
    xqT = carve(B + 45056, 8 * 512, BF16)

    Cb = OFF_C
    yTq = carve(Cb + 0, 8 * 512, BF16)
    Osb = carve(Cb + 8192, 1024, F32, parts=65)
    L2 = [carve(Cb + 12288, 512, F32, parts=2), carve(Cb + 14336, 512, F32, parts=2)]
    Linv = [carve(Cb + 16384, 512, F32, parts=2), carve(Cb + 18432, 512, F32, parts=2)]
    bcb = [[carve(Cb + 20480, 512, F32, parts=64), carve(Cb + 22528, 512, F32, parts=64)],
           [carve(Cb + 24576, 512, F32, parts=64), carve(Cb + 26624, 512, F32, parts=64)]]
    hc = [carve(Cb + 2048 * b, 512, F32) for b in range(2)]
    tc = [carve(Cb + 4096 + 2048 * b, 512, F32) for b in range(2)]
    u2c = [carve(Cb + 8192 + 2048 * b, 512, F32) for b in range(2)]
    u4c = [carve(Cb + 12288 + 2048 * b, 512, F32) for b in range(2)]
    Pc = [carve(Cb + 16384 + 1024 * b, 512, BF16) for b in range(2)]
    Qc = [carve(Cb + 18432 + 1024 * b, 512, BF16) for b in range(2)]
    R1c = [carve(Cb + 20480 + 1024 * b, 512, BF16) for b in range(2)]
    R2c = [carve(Cb + 22528 + 1024 * b, 512, BF16) for b in range(2)]
    XD = 10
    xqc = [carve(Cb + 24576 + 1024 * j, 512, BF16) for j in range(XD)]
    sg = [carve(Cb + 34816 + 2048 * b, 512, F32) for b in range(2)]
    hqT2 = [carve(Cb + 38912 + 1024 * b, 512, BF16) for b in range(2)]
    rsb2 = [carve(Cb + 40960 + 4096 * b, C, F32) for b in range(3)]
    xqT2 = carve(Cb + 53248, 8 * 512, BF16)
    qU42 = [carve(Cb + 28672 + 4096 * b, C, F32) for b in range(2)]
    qP2 = [carve(Cb + 36864 + 2048 * b, C, BF16) for b in range(2)]
    qQ2 = [carve(Cb + 40960 + 2048 * b, C, BF16) for b in range(2)]
    qR12 = [carve(Cb + 45056 + 2048 * b, C, BF16) for b in range(2)]
    qR22 = [carve(Cb + 49152 + 2048 * b, C, BF16) for b in range(2)]
    qXQ2 = [carve(Cb + 53248 + 2048 * b, C, BF16) for b in range(2)]

    st1 = {nm: [sb(f"p1_{nm}{b}", [128, 1], F32) for b in range(2)]
           for nm in ["sum", "ssq", "negm", "msq", "nmse", "lnv", "rstd"]}
    qbm = [sb(f"qbm{b}", [128, 64], F32) for b in range(2)]
    qsc = [sb(f"qsc{b}", [128, 64], F32) for b in range(4)]
    qs_ = [sb(f"qs{b}", [128, 64], F32) for b in range(4)]
    qln = qsc
    qr_ = [sb(f"qr{b}", [128, 64], F32) for b in range(4)]
    junkB = carve(OFF_B + 0, C, F32)
    ms2 = [sb(f"p2ms{b}", [128, 20], F32) for b in range(2)]
    lnm = [sb(f"p2lm{b}", [128, 20], F32) for b in range(2)]
    rsd = [sb(f"p2rs{b}", [128, 20], F32) for b in range(2)]
    q4bm = [sb(f"q4bm{b}", [128, 32], F32) for b in range(2)]
    q4sc = [sb(f"q4sc{b}", [128, 32], F32) for b in range(2)]
    q4s = [sb(f"q4s{b}", [128, 32], F32) for b in range(2)]
    q4ln = [sb(f"q4ln{b}", [128, 32], F32) for b in range(2)]
    q4r = [sb(f"q4r{b}", [128, 32], F32) for b in range(2)]

    def quant(bufs, b, w, src, deps, dep_prev_z, dep_prev_out, vec_scale=False):
        nb = w // 16
        tb, ub = bufs["t"][b][:, 0:w], bufs["u"][b][:, 0:w]
        Pb, Qb = bufs["P"][b][:, 0:w], bufs["Q"][b][:, 0:w]
        xq = bufs["xq"][b][:, 0:w]
        bm, scb, sqb, lnb, rqb = (bufs[k][b][:, 0:nb] for k in ["bm", "sc", "s", "ln", "r"])
        pz = [dep_prev_z] if dep_prev_z else []
        po = [dep_prev_out] if dep_prev_out else []

        t_bm = P.step("vector", deps + pz, lambda v: v.tensor_reduce(
            out=bm, in_=r16(src), axis=AX.X, op=OP.max, apply_absolute_value=True))
        if vec_scale:
            def vsc(v):
                v.drain()
                v.tensor_scalar(out=sqb, in0=bm, scalar1=1.0 / 6.0, scalar2=1e-12, op0=OP.mult, op1=OP.max)
                v.drain()
                return v.reciprocal(out=rqb, in_=sqb)
            t_qs = P.step("vector", [t_bm], vsc)
        else:
            t_qs = P.step("scalar", [t_bm], lambda sc: (
                sc.activation(out=scb, in_=bm, func=AF.Relu, scale=1.0 / 6.0),
                sc.activation(out=sqb, in_=scb, func=AF.Copy, bias=1e-12),
                sc.activation(out=lnb, in_=sqb, func=AF.Ln),
                sc.activation(out=rqb, in_=lnb, func=AF.Exp, scale=-1.0))[-1])
        if "R1" in bufs:
            u4b = bufs["u4"][b][:, 0:w]
            R1, R2 = bufs["R1"][b][:, 0:w], bufs["R2"][b][:, 0:w]
            t_t = P.step("vector", [t_qs], lambda v: (
                v.tensor_tensor(out=r16(tb), in0=r16(src), in1=bc3(rqb, nb, 16), op=OP.mult),
                v.tensor_scalar(out=ub, in0=tb, scalar1=2.0, scalar2=-2.0, op0=OP.min, op1=OP.max),
                v.tensor_scalar(out=u4b, in0=tb, scalar1=4.0, scalar2=-4.0, op0=OP.min, op1=OP.max))[-1])
            t_cp = P.step("scalar", [t_t] + po, lambda sc: (
                sc.activation(out=Pb, in_=ub, func=AF.Copy, bias=96.0),
                sc.activation(out=Qb, in_=ub, func=AF.Copy, scale=-1.0, bias=-192.0),
                sc.activation(out=R1, in_=u4b, func=AF.Copy, bias=192.0),
                sc.activation(out=R2, in_=u4b, func=AF.Copy, scale=-1.0, bias=-416.0),
                sc.activation(out=xq, in_=tb, func=AF.Copy, bias=320.0))[-1])
            t_z = P.step("vector", [t_cp], lambda v: (
                v.tensor_tensor(out=Pb, in0=Pb, in1=Qb, op=OP.add),
                v.tensor_tensor(out=R1, in0=R1, in1=R2, op=OP.add),
                v.tensor_tensor(out=R1, in0=R1, in1=xq, op=OP.add),
                v.tensor_tensor(out=Pb, in0=Pb, in1=R1, op=OP.add),
                v.tensor_tensor(out=r16(xq), in0=r16(Pb), in1=bc3(sqb, nb, 16), op=OP.mult))[-1])
            return t_z
        Rb = xq
        t_t = P.step("vector", [t_qs], lambda v: (
            v.tensor_tensor(out=r16(tb), in0=r16(src), in1=bc3(rqb, nb, 16), op=OP.mult),
            v.tensor_scalar(out=ub, in0=tb, scalar1=2.0, scalar2=-2.0, op0=OP.min, op1=OP.max))[-1])
        t_ca = P.step("scalar", [t_t], lambda sc: (
            sc.activation(out=Pb, in_=ub, func=AF.Copy, bias=96.0),
            sc.activation(out=Qb, in_=ub, func=AF.Copy, scale=-1.0, bias=-192.0))[-1])
        t_v1 = P.step("vector", [t_ca], lambda v: (
            v.tensor_tensor(out=Pb, in0=Pb, in1=Qb, op=OP.add),
            v.tensor_scalar(out=ub, in0=tb, scalar1=4.0, scalar2=-4.0, op0=OP.min, op1=OP.max))[-1])
        t_cb = P.step("scalar", [t_v1], lambda sc: (
            sc.activation(out=Qb, in_=ub, func=AF.Copy, bias=192.0),
            sc.activation(out=Rb, in_=ub, func=AF.Copy, scale=-1.0, bias=-416.0))[-1])
        t_v2 = P.step("vector", [t_cb], lambda v: v.tensor_tensor(out=Qb, in0=Qb, in1=Rb, op=OP.add))
        t_cc = P.step("scalar", [t_v2] + po, lambda sc: sc.activation(out=Rb, in_=tb, func=AF.Copy, bias=320.0))
        t_z = P.step("vector", [t_cc], lambda v: (
            v.tensor_tensor(out=Qb, in0=Qb, in1=Rb, op=OP.add),
            v.tensor_tensor(out=Pb, in0=Pb, in1=Qb, op=OP.add),
            v.tensor_tensor(out=r16(xq), in0=r16(Pb), in1=bc3(sqb, nb, 16), op=OP.mult))[-1])
        return t_z

    qb1 = {"t": qt_, "u": qu_, "u4": qU4, "P": qP, "Q": qQ, "R1": qR1, "R2": qR2, "xq": qXQ,
           "bm": qbm, "sc": qsc, "s": qs_, "ln": qln, "r": qr_}

    NSTG = 12
    tokt = {}

    def p1_stage(i, k):
        b2, b3, b4 = i % 2, i % 3, i % 4
        q4 = i % 4
        g = lambda ii, kk: tokt.get((ii, kk)) if ii >= 0 else None
        if k == 0:
            return P.dma("sync", [g(i - 3, 3)], xt[b3], x_d[128 * i:128 * (i + 1), :])
        if k == 1:
            return P.step("vector", [tokt[(i, 0)], g(i - 2, 2)], lambda v: v.tensor_reduce(
                out=st1["sum"][b2], in_=xt[b3], axis=AX.X, op=OP.add))
        if k == 2:
            return P.step("scalar", [tokt[(i, 0)], tokt[(i, 1)], g(i - 2, 3)], lambda sc: (
                sc.activation(out=st1["negm"][b2], in_=st1["sum"][b2], func=AF.Copy, scale=-1.0 / C),
                sc.activation(out=st1["msq"][b2], in_=st1["negm"][b2], func=AF.Square),
                sc.activation(out=st1["nmse"][b2], in_=st1["msq"][b2], func=AF.Copy, scale=-1.0, bias=1e-5),
                sc.activation(out=junkB, in_=xt[b3], func=AF.Square, accum_out=st1["ssq"][b2]),
                sc.activation(out=st1["lnv"][b2], in_=st1["ssq"][b2], func=AF.Ln, scale=1.0 / C, bias=st1["nmse"][b2]),
                sc.activation(out=st1["rstd"][b2], in_=st1["lnv"][b2], func=AF.Exp, scale=-0.5))[-1])
        if k == 3:
            return P.step("vector", [tokt[(i, 2)], g(i - 4, 5), g(i - 4, 7)], lambda v: v.tensor_scalar(
                out=xn[b4], in0=xt[b3], scalar1=st1["negm"][b2], scalar2=st1["rstd"][b2],
                op0=OP.add, op1=OP.mult))
        if k == 4:
            return P.step("vector", [tokt[(i, 3)], g(i - 2, 6)], lambda v: v.tensor_reduce(
                out=qbm[b2], in_=r16(xn[b4]), axis=AX.X, op=OP.max, apply_absolute_value=True))
        if k == 5:
            return P.dma("gpsimd", [tokt[(i, 3)]], xn_d[128 * i:128 * (i + 1), :], xn[b4])
        if k == 6:
            return P.step("scalar", [tokt[(i, 4)], g(i - 4, 10)], lambda sc: (
                sc.activation(out=qsc[q4], in_=qbm[b2], func=AF.Relu, scale=1.0 / 6.0),
                sc.activation(out=qs_[q4], in_=qsc[q4], func=AF.Copy, bias=1e-12),
                sc.activation(out=qln[q4], in_=qs_[q4], func=AF.Ln),
                sc.activation(out=qr_[q4], in_=qln[q4], func=AF.Exp, scale=-1.0))[-1])
        if k == 7:
            return P.step("vector", [tokt[(i, 6)], g(i - 2, 8)], lambda v: (
                v.tensor_tensor(out=r16(qt_[b2]), in0=r16(xn[b4]), in1=bc3(qr_[q4], 64, 16), op=OP.mult),
                v.tensor_scalar(out=qu_[b2], in0=qt_[b2], scalar1=2.0, scalar2=-2.0, op0=OP.min, op1=OP.max),
                v.tensor_scalar(out=qU4[b2], in0=qt_[b2], scalar1=4.0, scalar2=-4.0, op0=OP.min, op1=OP.max))[-1])
        if k == 8:
            return P.step("scalar", [tokt[(i, 7)], g(i - 2, 10), g(i - 2, 11)], lambda sc: (
                sc.activation(out=qP[b2], in_=qu_[b2], func=AF.Copy, bias=96.0),
                sc.activation(out=qQ[b2], in_=qu_[b2], func=AF.Copy, scale=-1.0, bias=-192.0),
                sc.activation(out=qR1[b2], in_=qU4[b2], func=AF.Copy, bias=192.0),
                sc.activation(out=qR2[b2], in_=qU4[b2], func=AF.Copy, scale=-1.0, bias=-416.0),
                sc.activation(out=qXQ[b2], in_=qt_[b2], func=AF.Copy, bias=320.0))[-1])
        if k == 9:
            return P.step("gpsimd", [tokt[(i, 8)]], lambda gp: (
                gp.tensor_tensor(out=qP[b2], in0=qP[b2], in1=qQ[b2], op=OP.add),
                gp.tensor_tensor(out=qR1[b2], in0=qR1[b2], in1=qR2[b2], op=OP.add),
                gp.tensor_tensor(out=qR1[b2], in0=qR1[b2], in1=qXQ[b2], op=OP.add),
                gp.tensor_tensor(out=qP[b2], in0=qP[b2], in1=qR1[b2], op=OP.add))[-1])
        if k == 10:
            return P.step("vector", [tokt[(i, 9)]], lambda v: v.tensor_tensor(
                out=r16(qXQ[b2]), in0=r16(qP[b2]), in1=bc3(qs_[q4], 64, 16), op=OP.mult))
        if k == 11:
            return P.dma("sync", [tokt[(i, 10)]], xq_d[128 * i:128 * (i + 1), :], qXQ[b2])

    P1_SCHED = [(10, 10), (11, 10), (9, 9), (8, 8), (7, 7), (6, 6),
                (5, 5), (4, 4), (3, 3), (2, 2), (1, 1), (0, 0)]
    for tau in range(NT + NSTG - 1):
        for k, off in P1_SCHED:
            i = tau - off
            if 0 <= i < NT:
                tokt[(i, k)] = p1_stage(i, k)
    tok_xnv = [tokt[(i, 3)] for i in range(NT)]
    tok_xndma = [tokt[(i, 5)] for i in range(NT)]
    tok_xqdma = [tokt[(i, 11)] for i in range(NT)]
    tok_ln1 = [tokt[(i, 2)] for i in range(NT)]
    tok_z1 = [tokt[(i, 10)] for i in range(NT)]

    wtiles = ([("g", (i, 0), C) for i in range(22)] + [("u", (i, 0), C) for i in range(22)]
              + [("d", (i, cc), (1024 if cc < 2 else 768)) for i in range(8) for cc in range(3)])
    NWS = 8
    tokw = {}
    p1_last = tokt[(NT - 1, 11)]

    def p1b_stage(n, k):
        kind, (i, cc), w = wtiles[n]
        b2, b3 = n % 2, n % 3
        q4 = n % 4
        nb = w // 16
        srcd = {"g": wgate_d, "u": wup_d, "d": wdown_d}[kind]
        dstd = {"g": wgq_d, "u": wuq_d, "d": wdq_d}[kind]
        sl_src = srcd[128 * i:128 * (i + 1), 1024 * cc:1024 * cc + w]
        sl_dst = dstd[128 * i:128 * (i + 1), 1024 * cc:1024 * cc + w]

        def g(nn, kk):
            if nn >= 0:
                return tokw.get((nn, kk))
            return p1_last
        if k == 0:
            return P.dma("sync", [g(n - 3, 3)], wtb2[b3][:, 0:w], sl_src)
        if k == 1:
            return P.step("vector", [tokw[(n, 0)], g(n - 2, 2)], lambda v: v.tensor_reduce(
                out=qbm[b2][:, 0:nb], in_=r16(wtb2[b3][:, 0:w]), axis=AX.X, op=OP.max,
                apply_absolute_value=True))
        if k == 2:
            return P.step("scalar", [tokw[(n, 1)], g(n - 4, 6)], lambda sc: (
                sc.activation(out=qsc[q4][:, 0:nb], in_=qbm[b2][:, 0:nb], func=AF.Relu, scale=1.0 / 6.0),
                sc.activation(out=qs_[q4][:, 0:nb], in_=qsc[q4][:, 0:nb], func=AF.Copy, bias=1e-12),
                sc.activation(out=qln[q4][:, 0:nb], in_=qs_[q4][:, 0:nb], func=AF.Ln),
                sc.activation(out=qr_[q4][:, 0:nb], in_=qln[q4][:, 0:nb], func=AF.Exp, scale=-1.0))[-1])
        if k == 3:
            return P.step("vector", [tokw[(n, 2)], g(n - 2, 4)], lambda v: (
                v.tensor_tensor(out=r16(qt2[b2][:, 0:w]), in0=r16(wtb2[b3][:, 0:w]),
                                in1=bc3(qr_[q4][:, 0:nb], nb, 16), op=OP.mult),
                v.tensor_scalar(out=qu2[b2][:, 0:w], in0=qt2[b2][:, 0:w], scalar1=2.0, scalar2=-2.0,
                                op0=OP.min, op1=OP.max),
                v.tensor_scalar(out=qU42[b2][:, 0:w], in0=qt2[b2][:, 0:w], scalar1=4.0, scalar2=-4.0,
                                op0=OP.min, op1=OP.max))[-1])
        if k == 4:
            return P.step("scalar", [tokw[(n, 3)], g(n - 2, 6), g(n - 2, 7)], lambda sc: (
                sc.activation(out=qP2[b2][:, 0:w], in_=qu2[b2][:, 0:w], func=AF.Copy, bias=96.0),
                sc.activation(out=qQ2[b2][:, 0:w], in_=qu2[b2][:, 0:w], func=AF.Copy, scale=-1.0, bias=-192.0),
                sc.activation(out=qR12[b2][:, 0:w], in_=qU42[b2][:, 0:w], func=AF.Copy, bias=192.0),
                sc.activation(out=qR22[b2][:, 0:w], in_=qU42[b2][:, 0:w], func=AF.Copy, scale=-1.0, bias=-416.0),
                sc.activation(out=qXQ2[b2][:, 0:w], in_=qt2[b2][:, 0:w], func=AF.Copy, bias=320.0))[-1])
        if k == 5:
            return P.step("gpsimd", [tokw[(n, 4)]], lambda gp: (
                gp.tensor_tensor(out=qP2[b2][:, 0:w], in0=qP2[b2][:, 0:w], in1=qQ2[b2][:, 0:w], op=OP.add),
                gp.tensor_tensor(out=qR12[b2][:, 0:w], in0=qR12[b2][:, 0:w], in1=qR22[b2][:, 0:w], op=OP.add),
                gp.tensor_tensor(out=qR12[b2][:, 0:w], in0=qR12[b2][:, 0:w], in1=qXQ2[b2][:, 0:w], op=OP.add),
                gp.tensor_tensor(out=qP2[b2][:, 0:w], in0=qP2[b2][:, 0:w], in1=qR12[b2][:, 0:w], op=OP.add))[-1])
        if k == 6:
            return P.step("vector", [tokw[(n, 5)]], lambda v: v.tensor_tensor(
                out=r16(qXQ2[b2][:, 0:w]), in0=r16(qP2[b2][:, 0:w]),
                in1=bc3(qs_[q4][:, 0:nb], nb, 16), op=OP.mult))
        if k == 7:
            return P.dma("gpsimd", [tokw[(n, 6)]], sl_dst, qXQ2[b2][:, 0:w])

    P1B_SCHED = [(6, 6), (7, 6), (5, 5), (4, 4), (3, 3), (2, 2), (1, 1), (0, 0)]
    p1b_entries = []
    for tau in range(len(wtiles) + NWS - 1):
        for k, off in P1B_SCHED:
            n = tau - off
            if 0 <= n < len(wtiles):
                p1b_entries.append((n, k))

    def emit_p1b(cnt):
        while cnt > 0 and p1b_entries:
            n, k = p1b_entries.pop(0)
            tokw[(n, k)] = p1b_stage(n, k)
            cnt -= 1

    emit_p1b(len(p1b_entries))
    tok_wq_last = tokw[(len(wtiles) - 1, 7)]

    pq = [psA[:, 0:QKV], psB[:, 0:QKV]]
    tok_cos = P.dma("sync", [tok_ln1[NT - 1]], cosb.rearrange("p (n f) -> p n f", f=HD), cos_d[:, :].rearrange("(n p) f -> p n f", p=128))
    tok_sin = P.dma("sync", [tok_ln1[NT - 1]], sinb.rearrange("p (n f) -> p n f", f=HD), sin_d[:, :].rearrange("(n p) f -> p n f", p=128))
    tok_p1q = [tok_z1[NT - 1]]
    tok_wa = [P.dma("gpsimd", tok_p1q, wa[:, QKV * cb:QKV * (cb + 1)],
                    wattn_d[128 * cb:128 * (cb + 1), :]) for cb in range(8)]
    t_v65i = P.step("gpsimd", tok_p1q, lambda g: g.memset(v65, 1.0))

    tok_xnTg = [None] * 8
    tok_mm2 = [None] * NT
    tok_rope = [None] * NT
    tok_vcp = [None] * NT
    tok_qnv = [None] * NT
    tok_qdma = [None] * NT
    tok_kdma = [None] * NT
    tok_sq2 = [None] * NT
    tok_ms = [None] * NT
    tok_p1x = [tokt[(NT - 1, 3)], tokt[(28, 7)]]
    for i in range(NT):
        b = i % 2
        pr = i - 2
        grp, ti = divmod(i, 4)
        if ti == 0:
            dep = [tok_xndma[4 * grp + k] for k in range(4)] + tok_p1x
            if grp >= 2:
                dep.append(tok_mm2[4 * (grp - 2) + 3])
            tok_xnTg[grp] = [P.dma("sync", dep, xnTg[grp % 2][:, 512 * cb:512 * (cb + 1)],
                                   xn_d[512 * grp:512 * (grp + 1), 128 * cb:128 * (cb + 1)],
                                   transpose=True) for cb in range(8)]

        def mmfn(t, b=b, ti=ti, gb=grp % 2):
            for nch in range(3):
                for cb in range(8):
                    ins = t.matmul(out=pq[b][:, 512 * nch:512 * (nch + 1)],
                                   lhsT=xnTg[gb][:, 512 * cb + 128 * ti:512 * cb + 128 * (ti + 1)],
                                   rhs=wa[:, QKV * cb + 512 * nch:QKV * cb + 512 * (nch + 1)],
                                   start=(cb == 0), stop=(cb == 7))
            return ins
        depm = tok_xnTg[grp] + tok_wa
        if pr >= 0:
            depm = depm + [tok_rope[pr], tok_vcp[pr]]
        tok_mm2[i] = P.step("tensor", depm, mmfn)

        def ropefn(v, b=b, i=i):
            co = HD * i
            pst = list(pq[b].ap[0])
            pofs = pq[b].offset
            cview = cosb[:, co:co + HD]
            sview = sinb[:, co:co + HD]
            v.tensor_tensor(out=m1[:, 0:C].rearrange("p (h d) -> p h d", d=HD),
                            in0=pq[b][:, 0:C].rearrange("p (h d) -> p h d", d=HD),
                            in1=cview.unsqueeze(1).broadcast_to([128, NH, HD]), op=OP.mult)
            v.tensor_tensor(out=m2[:, 0:C].rearrange("p (h d) -> p h d", d=HD),
                            in0=bass.AP(pq[b].tensor, pofs + 1, [pst, [HD, NH], [2, HD // 2], [-1, 2]]),
                            in1=sview.unsqueeze(1).broadcast_to([128, NH, HD]), op=OP.mult)
            v.tensor_tensor(out=m1[:, C:C + KVD].rearrange("p (h d) -> p h d", d=HD),
                            in0=pq[b][:, C:C + KVD].rearrange("p (h d) -> p h d", d=HD),
                            in1=cview.unsqueeze(1).broadcast_to([128, NKV, HD]), op=OP.mult)
            v.tensor_tensor(out=m2[:, C:C + KVD].rearrange("p (h d) -> p h d", d=HD),
                            in0=bass.AP(pq[b].tensor, pofs + C + 1, [pst, [HD, NKV], [2, HD // 2], [-1, 2]]),
                            in1=sview.unsqueeze(1).broadcast_to([128, NKV, HD]), op=OP.mult)
            return v.tensor_tensor(out=rot[b], in0=m1, in1=m2, op=OP.add)
        rdeps = [tok_mm2[i], tok_cos, tok_sin]
        if i >= 1:
            rdeps.append(tok_sq2[i - 1])
        if pr >= 0:
            rdeps.append(tok_qnv[pr])
        tok_rope[i] = P.step("vector", rdeps, ropefn)

        def vcpfn(v, b=b, i=i):
            vdst = v65[:, 260 * i:260 * (i + 1)].rearrange("p (g u) -> p g u", u=65)[:, :, 0:64]
            return v.tensor_scalar(out=vdst, in0=pq[b][:, C + KVD:QKV].rearrange("p (g d) -> p g d", d=64),
                                   scalar1=1.0, scalar2=None, op0=OP.mult)
        tok_vcp[i] = P.step("vector", [tok_mm2[i], t_v65i], vcpfn)

        tok_sq2[i] = P.step("scalar", [tok_rope[i]] + ([tok_ms[i - 1]] if i >= 1 else []),
                            lambda sc, b=b: sc.activation(out=sqq, in_=rot[b], func=AF.Square))
        tok_ms[i] = P.step("vector", [tok_sq2[i]], lambda v, b=b: v.tensor_reduce(
            out=ms2[b], in_=sqq.rearrange("p (h d) -> p h d", d=HD), axis=AX.X, op=OP.add))
        t_rms = P.step("scalar", [tok_ms[i]], lambda sc, b=b: (
            sc.activation(out=lnm[b], in_=ms2[b], func=AF.Copy, scale=1.0 / HD, bias=1e-6),
            sc.activation(out=lnm[b], in_=lnm[b], func=AF.Ln),
            sc.activation(out=rsd[b], in_=lnm[b], func=AF.Exp, scale=-0.5))[-1])

        def qnfn(v, b=b):
            v.tensor_tensor(out=qn[b].rearrange("p (h d) -> p h d", d=HD),
                            in0=rot[b][:, 0:C].rearrange("p (h d) -> p h d", d=HD),
                            in1=bc3(rsd[b][:, 0:NH], NH, HD), op=OP.mult)
            return v.tensor_tensor(out=kn[b].rearrange("p (h d) -> p h d", d=HD),
                                   in0=rot[b][:, C:C + KVD].rearrange("p (h d) -> p h d", d=HD),
                                   in1=bc3(rsd[b][:, NH:NH + NKV], NKV, HD), op=OP.mult)
        tok_qnv[i] = P.step("vector", [t_rms] + ([tok_qdma[pr], tok_kdma[pr]] if pr >= 0 else []), qnfn)
        tok_qdma[i] = P.dma("sync", [tok_qnv[i]], q_d[128 * i:128 * (i + 1), :], qn[b])
        kd3 = k_d[128 * i:128 * (i + 1), :].rearrange("t (g dd) -> t g dd", dd=128)
        ks3 = kn[b].rearrange("t (g d) -> t g d", d=64)
        P.dma("sync", [tok_qnv[i]], kd3[:, :, 0:64], ks3)
        tok_kdma[i] = P.dma("sync", [tok_qnv[i]], kd3[:, :, 64:128], ks3)

    Spp = [psA[:, 0:1024], psA[:, 1024:2048], psB[:, 1024:2048]]
    O_ps = psB[:, 0:1024]

    n_it = 0
    tok_S = {}
    tok_exp = {}
    tok_avp = {}
    tok_oev = None
    tok_yt = None
    tok_hl_slot = [None, None]
    tok_ytd = [[None] * NQC for _ in range(8)]
    tok_ytd_prev = None
    tok_bcb = [None, None]
    pend_av = None
    lslot = 0

    for g in range(4):
        gb = g % 2
        depq = [tok_qdma[NT - 1]]
        if g >= 2:
            depq.append(tok_S[("glast", g - 2)])
        tq0 = P.dma("sync", depq, qTg[gb][:, 0:T], q_d[:, 256 * g:256 * g + 128], transpose=True)
        tq1 = P.dma("sync", depq, qTg[gb][:, T:2 * T], q_d[:, 256 * g + 128:256 * g + 256], transpose=True)
        depk = [tok_kdma[NT - 1]]
        if g >= 1:
            depk.append(tok_S[("glast", g - 1)])
        tk0 = P.dma("sync", depk, kTd, k_d[:, 128 * g:128 * (g + 1)], transpose=True)
        gload = [tq0, tq1, tk0]

        for j in range(2):
            pblk = T * j
            for qc in range(NQC):
                nkt = 4 * qc + 4
                for kt in range(nkt):
                    n = n_it
                    n_it += 1
                    sbi = n % 3
                    diag = kt >= 4 * qc
                    off = 128 * (kt - 4 * qc) if diag else 0

                    def Sfn(t, gb=gb, sbi=sbi, pblk=pblk, qc=qc, kt=kt, off=off, diag=diag):
                        t.matmul(out=Spp[sbi][:, off:512],
                                 lhsT=kTd[0:64, 128 * kt:128 * (kt + 1)],
                                 rhs=qTg[gb][0:64, pblk + 512 * qc + off:pblk + 512 * (qc + 1)],
                                 start=True, stop=not diag, tile_position=(0, 0))
                        ins = t.matmul(out=Spp[sbi][:, 512 + off:1024],
                                       lhsT=kTd[64:128, 128 * kt:128 * (kt + 1)],
                                       rhs=qTg[gb][64:128, pblk + 512 * qc + off:pblk + 512 * (qc + 1)],
                                       start=True, stop=not diag, tile_position=(64, 0))
                        if diag:
                            t.matmul(out=Spp[sbi][:, off:off + 128], lhsT=sel, rhs=tri,
                                     start=False, stop=True, skip_group_check=True)
                            ins = t.matmul(out=Spp[sbi][:, 512 + off:512 + off + 128], lhsT=sel, rhs=tri,
                                           start=False, stop=True, skip_group_check=True)
                        return ins
                    depS = gload + [t_sel, t_tri] + ([tok_exp[n - 3]] if n >= 3 else [])
                    if kt == 0 and tok_oev is not None:
                        depS = depS + [tok_oev]
                    tok_S[n] = P.step("tensor", depS, Sfn)

                    def expfn(sc, sbi=sbi, off=off):
                        pv = PT[sbi].rearrange("p (h w) -> p h w", h=2)[:, :, off:512]
                        sv = Spp[sbi].rearrange("p (h w) -> p h w", h=2)[:, :, off:512]
                        return sc.activation(out=pv, in_=sv, func=AF.Exp, scale=0.125)
                    depE = [tok_S[n]] + ([tok_avp[n - 3]] if n >= 3 else [])
                    tok_exp[n] = P.step("scalar", depE, expfn)

                    dep_av = [tok_exp[n], t_v65i]
                    if kt == 0 and tok_oev is not None:
                        dep_av = dep_av + [tok_oev]

                    def avfn(t, sbi=sbi, kt=kt, g=g, off=off, nkt=nkt):
                        for r in range(2):
                            ins = t.matmul(out=O_ps[0:65, 512 * r + off:512 * (r + 1)],
                                           lhsT=v65[:, 260 * kt + 65 * g:260 * kt + 65 * (g + 1)],
                                           rhs=PT[sbi][:, 512 * r + off:512 * (r + 1)],
                                           start=(kt == 0), stop=(kt == nkt - 1))
                        return ins
                    if pend_av is not None:
                        pn, pfn, pdep = pend_av
                        tok_avp[pn] = P.step("tensor", pdep, pfn)
                    pend_av = (n, avfn, dep_av)

                pn, pfn, pdep = pend_av
                tok_avp[pn] = P.step("tensor", pdep, pfn)
                pend_av = None

                tok_oev = P.step("vector", [tok_avp[n_it - 1]] + ([tok_yt] if tok_yt else []),
                                 lambda v: v.tensor_copy(out=Osb, in_=O_ps[0:65, :]))
                sl = lslot % 2
                lslot += 1
                lsrc = Osb[64:65, :].rearrange("p (a b) -> p a b", b=512)
                t_lo = P.dma("gpsimd", [tok_oev] + ([tok_hl_slot[sl]] if tok_hl_slot[sl] else []),
                             l_d[2 * sl:2 * (sl + 1), :], lsrc)
                t_li = P.dma("gpsimd", [t_lo], L2[sl], l_d[2 * sl:2 * (sl + 1), :])
                t_rc = P.step("vector", [t_li], lambda v, sl=sl: v.reciprocal(out=Linv[sl], in_=L2[sl]))
                tok_hl_slot[sl] = t_rc
                t_lo2 = P.dma("gpsimd", [t_rc], linv_d[2 * sl:2 * (sl + 1), :], Linv[sl])
                bsl = lslot % 2
                bdep = [t_lo2] + ([tok_bcb[bsl]] if tok_bcb[bsl] else [])
                bc0src = bass.AP(linv_d, 2 * sl * 512, [[0, 64], [1, 512]])
                bc1src = bass.AP(linv_d, (2 * sl + 1) * 512, [[0, 64], [1, 512]])
                P.dma("gpsimd", bdep, bcb[bsl][0], bc0src)
                t_bc = P.dma("gpsimd", bdep, bcb[bsl][1], bc1src)

                def ytfn(v, bsl=bsl):
                    v.tensor_tensor(out=ybuf[0:64, :], in0=Osb[0:64, 0:512],
                                    in1=bcb[bsl][0], op=OP.mult)
                    return v.tensor_tensor(out=ybuf[64:128, :], in0=Osb[0:64, 512:1024],
                                           in1=bcb[bsl][1], op=OP.mult)
                tok_yt = P.step("vector", [t_bc, tok_oev, tok_ytd_prev], ytfn)
                tok_bcb[bsl] = tok_yt
                pairp = 2 * g + j
                tok_ytd[pairp][qc] = P.dma("sync", [tok_yt],
                                           yt_d[128 * pairp:128 * (pairp + 1), 512 * qc:512 * (qc + 1)],
                                           ybuf)
                tok_ytd_prev = tok_ytd[pairp][qc]
        tok_S[("glast", g)] = tok_S[n_it - 1]

    pj = [psA[:, 0:1024], psA[:, 1024:2048]]
    tok_wp = [P.dma("gpsimd", [tok_mm2[NT - 1]], wp[:, C * cb:C * (cb + 1)],
                    wproj_d[128 * cb:128 * (cb + 1), :]) for cb in range(8)]
    tok_yq = [None] * NQC
    tok_pj = [None] * NT
    tok_ev3 = [None] * NT
    tok_rdma = [None] * NT
    for it in range(NT):
        b = it % 2
        b3b = it % 3
        pr = it - 2
        pr3 = it - 3
        qc, ts = divmod(it, 4)
        if ts == 0:
            dep = [tok_ytd[p][qc] for p in range(8)]
            if qc >= 1:
                dep.append(tok_pj[4 * (qc - 1) + 3])
            tok_yq[qc] = [P.dma("sync", dep, yTq[:, 512 * p:512 * (p + 1)],
                                yt_d[128 * p:128 * (p + 1), 512 * qc:512 * (qc + 1)])
                          for p in range(8)]
        t_x3 = P.dma("sync", [tok_ev3[pr3], tok_mm2[NT - 1]] if pr3 >= 0 else [tok_mm2[NT - 1]],
                     xb3[b3b], x_d[128 * it:128 * (it + 1), :])

        def pjfn(t, b=b, qc=qc, ts=ts):
            for ch in range(2):
                for p in range(8):
                    ins = t.matmul(out=pj[b][:, 512 * ch:512 * (ch + 1)],
                                   lhsT=yTq[:, 512 * p + 128 * ts:512 * p + 128 * (ts + 1)],
                                   rhs=wp[:, C * p + 512 * ch:C * p + 512 * (ch + 1)],
                                   start=(p == 0), stop=(p == 7))
            return ins
        depj = tok_yq[qc] + tok_wp + [tok_exp[n_it - 1] if it < 2 else tok_ev3[pr]]
        tok_pj[it] = P.step("tensor", depj, pjfn)
        tok_ev3[it] = P.step("vector", [tok_pj[it], t_x3] + ([tok_rdma[pr3]] if pr3 >= 0 else []),
                             lambda v, b=b, b3b=b3b: v.tensor_tensor(
                                 out=rs3[b3b], in0=pj[b], in1=xb3[b3b], op=OP.add))
        tok_rdma[it] = P.dma("sync" if it % 2 == 0 else "gpsimd", [tok_ev3[it]],
                             resid_d[128 * it:128 * (it + 1), :], rs3[b3b])

    ppg = [psA[:, 0:512], psA[:, 512:1024]]
    ppu = [psA[:, 1024:1536], psA[:, 1536:2048]]
    pdn = psB[:, 0:1024]
    pht = [psB[:, 1024:1280].bitcast(BF16), psB[:, 1536:1792].bitcast(BF16)]

    depA4 = [tok_wq_last, tok_pj[NT - 1], tok_rdma[NT - 1], tok_rdma[NT - 2], tok_ev3[NT - 1]]
    tok_wgT = P.dma_batch("sync", depA4, [(wgT[:, I * cb:I * (cb + 1)],
                          wgq_d[:, 128 * cb:128 * (cb + 1)]) for cb in range(8)], transpose=True)
    tok_wuT = P.dma_batch("sync", depA4, [(wuT[:, I * cb:I * (cb + 1)],
                          wuq_d[:, 128 * cb:128 * (cb + 1)]) for cb in range(8)], transpose=True)
    depB4 = [tok_wq_last, tok_yt, tok_xqdma[NT - 1], tok_ytd[7][NQC - 1]]
    tok_wdT = P.dma_batch("sync", depB4, [(wdT[:, C * ib:C * (ib + 1)],
                          wdq_d[:, 128 * ib:128 * (ib + 1)]) for ib in range(22)], transpose=True)
    w4loads = [tok_wgT, tok_wuT, tok_wdT]
    xqTb = [xqT, xqT2]

    tok_xqT = [None] * 8
    tg, tu, tsil, thm, tbm, tqs, ttq, tcp, tz = {}, {}, {}, {}, {}, {}, {}, {}, {}
    ttp, the, tdn = {}, {}, {}
    tok_out = [None] * NT
    tok_ydma = [None] * NT
    tok_r4 = [None] * NT
    tok_grp_consumed = [None] * 8

    def slot_of(cg):
        return divmod(cg, NIC) if cg >= 0 else None

    NCG = NT * NIC
    LAG, DLAG = 9, 10

    def emit_tphe(m_it, ic):
        w, io, bb = IC_W[ic], IC_OFF[ic], ic % 2
        nsub = w // 128
        m = (m_it, ic)
        cg = m_it * NIC + ic
        pm2 = slot_of(cg - 2)

        xj = cg % XD

        def tpfn(t, xj=xj, bb=bb, nsub=nsub):
            for ss in range(nsub):
                ins = t.transpose(out=pht[bb][:, 128 * ss:128 * (ss + 1)],
                                  in_=xqc[xj][:, 128 * ss:128 * (ss + 1)], identity=idn)
            return ins
        deptp = [tz[m], t_idn] + ([the[pm2]] if pm2 and pm2[0] >= 0 else [])
        ttp[m] = P.step("tensor", deptp, tpfn)
        the[m] = P.step("scalar", [ttp[m]] + ([tdn[pm2]] if pm2 and pm2[0] >= 0 else []),
                        lambda sc, bb=bb, w=w: sc.activation(
            out=hqT2[bb][:, 0:w], in_=pht[bb][:, 0:w], func=AF.Copy))

    def emit_dn(m_it, ic):
        w, io, bb = IC_W[ic], IC_OFF[ic], ic % 2
        nsub = w // 128
        m = (m_it, ic)

        def dnfn(t, ic=ic, io=io, bb=bb, nsub=nsub):
            for ss in range(nsub):
                for ch in range(2):
                    ins = t.matmul(out=pdn[:, 512 * ch:512 * (ch + 1)],
                                   lhsT=hqT2[bb][:, 128 * ss:128 * (ss + 1)],
                                   rhs=wdT[:, C * (io // 128 + ss) + 512 * ch:C * (io // 128 + ss) + 512 * (ch + 1)],
                                   start=(ic == 0 and ss == 0), stop=(ic == NIC - 1 and ss == nsub - 1))
            return ins
        depdn = [the[m]]
        if ic == 0:
            depdn += [tok_out[m_it - 1]] if m_it >= 1 else [tok_yt, tok_oev]
        tdn[m] = P.step("tensor", depdn, dnfn)
        if ic == NIC - 1:
            tok_out[m_it] = P.step("vector", [tdn[m], tok_r4[m_it]],
                                   lambda v, b=m_it % 3: v.tensor_tensor(
                out=rsb2[b], in0=pdn, in1=rsb2[b], op=OP.add))
            tok_ydma[m_it] = P.dma("sync", [tok_out[m_it]],
                                   y_d[128 * m_it:128 * (m_it + 1), :], rsb2[m_it % 3])

    for cg in range(NCG + DLAG + 1):
        if cg < NCG:
            it, ic = divmod(cg, NIC)
            grp, ti = divmod(it, 4)
            w, io, bb = IC_W[ic], IC_OFF[ic], ic % 2
            nb = w // 16
            n = (it, ic)
            p2 = slot_of(cg - 2)
            if ic == 0:
                if it == 0:
                    dep = [tok_xqdma[NT - 1], tok_yt, tok_ytd[7][NQC - 1]]
                    tok_xqT[0] = P.dma_batch("sync", dep, [
                        (xqTb[0][:, 512 * cb:512 * (cb + 1)], xq_d[0:512, 128 * cb:128 * (cb + 1)])
                        for cb in range(8)], transpose=True)
                if ti == 2 and grp + 1 < 8:
                    g2 = grp + 1
                    depp = [tok_xqdma[NT - 1]] + ([tok_grp_consumed[g2 - 2]] if g2 >= 2 else [])
                    tok_xqT[g2] = P.dma_batch("sync", depp, [
                        (xqTb[g2 % 2][:, 512 * cb:512 * (cb + 1)],
                         xq_d[512 * g2:512 * (g2 + 1), 128 * cb:128 * (cb + 1)])
                        for cb in range(8)], transpose=True)

        mtp = slot_of(cg - LAG)
        if mtp is not None and mtp[0] < NT:
            emit_tphe(*mtp)
        mdn = slot_of(cg - DLAG)
        if mdn is not None and mdn[0] < NT:
            emit_dn(*mdn)
        if cg < NCG and ic == 0:
            assert it < 3 or tok_ydma[it - 3] is not None
            tok_r4[it] = P.dma("sync", [tok_rdma[it]]
                               + ([tok_ydma[it - 3]] if it >= 3 else [tok_wq_last]),
                               rsb2[it % 3], resid_d[128 * it:128 * (it + 1), :])
        if cg < NCG:
            def gufn(t, ti=ti, io=io, w=w, bb=bb, gp=grp % 2, which=0):
                wT = wgT if which == 0 else wuT
                pp = ppg if which == 0 else ppu
                for cb in range(8):
                    ins = t.matmul(out=pp[bb][:, 0:w],
                                   lhsT=xqTb[gp][:, 512 * cb + 128 * ti:512 * cb + 128 * (ti + 1)],
                                   rhs=wT[:, I * cb + io:I * cb + io + w],
                                   start=(cb == 0), stop=(cb == 7))
                return ins
            depg = w4loads + [tok_xqT[grp]]
            if p2 is not None:
                depg = depg + [tsil[p2]]
            if it == 0 and ic == 0:
                depg = depg + [tok_ev3[NT - 1], tok_ev3[NT - 2], tok_pj[NT - 1],
                               tok_yt, tok_oev, tok_ytd[7][NQC - 1]]
            tg[n] = P.step("tensor", depg, lambda t, gufn=gufn: gufn(t, which=0))
            tu[n] = P.step("tensor", [thm[p2]] if p2 else [], lambda t, gufn=gufn: gufn(t, which=1))

        if cg < NCG:
            tsil[n] = P.step("scalar", [tg[n]] + ([thm[p2]] if p2 else []),
                             lambda sc, bb=bb, w=w: sc.activation(
                out=sg[bb][:, 0:w], in_=ppg[bb][:, 0:w], func=AF.Silu))
            thm[n] = P.step("vector", [tsil[n], tu[n]] + ([ttq[p2]] if p2 else []),
                            lambda v, bb=bb, w=w: v.tensor_tensor(
                out=hc[bb][:, 0:w], in0=sg[bb][:, 0:w], in1=ppu[bb][:, 0:w], op=OP.mult))
            tbm[n] = P.step("vector", [thm[n]] + ([tqs[p2]] if p2 else []),
                            lambda v, bb=bb, w=w, nb=nb: v.tensor_reduce(
                out=q4bm[bb][:, 0:nb], in_=r16(hc[bb][:, 0:w]), axis=AX.X, op=OP.max,
                apply_absolute_value=True))
            tqs[n] = P.step("scalar", [tbm[n]] + ([tz[p2]] if p2 else []),
                            lambda sc, bb=bb, nb=nb: (
                sc.activation(out=q4sc[bb][:, 0:nb], in_=q4bm[bb][:, 0:nb], func=AF.Relu, scale=1.0 / 6.0),
                sc.activation(out=q4s[bb][:, 0:nb], in_=q4sc[bb][:, 0:nb], func=AF.Copy, bias=1e-12))[-1])
            ttq[n] = P.step("vector", [tqs[n]] + ([tcp[p2]] if p2 else []),
                            lambda v, bb=bb, w=w, nb=nb: (
                v.reciprocal(out=q4r[bb][:, 0:nb], in_=q4s[bb][:, 0:nb]),
                v.drain(),
                v.tensor_tensor(out=r16(tc[bb][:, 0:w]), in0=r16(hc[bb][:, 0:w]),
                                in1=bc3(q4r[bb][:, 0:nb], nb, 16), op=OP.mult),
                v.tensor_scalar(out=u2c[bb][:, 0:w], in0=tc[bb][:, 0:w], scalar1=2.0, scalar2=-2.0,
                                op0=OP.min, op1=OP.max),
                v.tensor_scalar(out=u4c[bb][:, 0:w], in0=tc[bb][:, 0:w], scalar1=4.0, scalar2=-4.0,
                                op0=OP.min, op1=OP.max))[-1])
            depcp = [ttq[n]] + ([tz[p2]] if p2 else [])
            if cg >= XD:
                depcp.append(ttp[slot_of(cg - XD)])
            tcp[n] = P.step("scalar", depcp, lambda sc, bb=bb, w=w, xj=cg % XD: (
                sc.activation(out=Pc[bb][:, 0:w], in_=u2c[bb][:, 0:w], func=AF.Copy, bias=96.0),
                sc.activation(out=Qc[bb][:, 0:w], in_=u2c[bb][:, 0:w], func=AF.Copy, scale=-1.0, bias=-192.0),
                sc.activation(out=R1c[bb][:, 0:w], in_=u4c[bb][:, 0:w], func=AF.Copy, bias=192.0),
                sc.activation(out=R2c[bb][:, 0:w], in_=u4c[bb][:, 0:w], func=AF.Copy, scale=-1.0, bias=-416.0),
                sc.activation(out=xqc[xj][:, 0:w], in_=tc[bb][:, 0:w], func=AF.Copy, bias=320.0))[-1])
            tz[n] = P.step("vector", [tcp[n]], lambda v, bb=bb, w=w, xj=cg % XD, nb=nb: (
                v.tensor_tensor(out=Pc[bb][:, 0:w], in0=Pc[bb][:, 0:w], in1=Qc[bb][:, 0:w], op=OP.add),
                v.tensor_tensor(out=R1c[bb][:, 0:w], in0=R1c[bb][:, 0:w], in1=R2c[bb][:, 0:w], op=OP.add),
                v.tensor_tensor(out=R1c[bb][:, 0:w], in0=R1c[bb][:, 0:w], in1=xqc[xj][:, 0:w], op=OP.add),
                v.tensor_tensor(out=Pc[bb][:, 0:w], in0=Pc[bb][:, 0:w], in1=R1c[bb][:, 0:w], op=OP.add),
                v.tensor_tensor(out=r16(xqc[xj][:, 0:w]), in0=r16(Pc[bb][:, 0:w]),
                                in1=bc3(q4s[bb][:, 0:nb], nb, 16), op=OP.mult))[-1])
            if ic == NIC - 1:
                tok_grp_consumed[grp] = tu[n]

    with nc.Block() as block:
        P.emit(block)
    return nc


def _host_consts():
    f32 = np.float32
    pos = np.arange(T, dtype=f32)
    inv = (1.0 / (10000.0 ** (np.arange(0, HD, 2, dtype=f32) / f32(HD)))).astype(f32)
    fr = pos[:, None] * inv[None, :]
    cos2 = np.repeat(np.cos(fr).astype(f32), 2, axis=1).astype(f32)
    sin2 = np.repeat(np.sin(fr).astype(f32), 2, axis=1).astype(f32)
    sin2[:, 0::2] *= -1.0
    a = np.arange(128)
    tri = (a[None, :] < a[:, None]).astype(bfdt)
    sel = (np.eye(128) * -320.0).astype(bfdt)
    idn = np.eye(128).astype(bfdt)
    return cos2, sin2, tri, sel, idn


def kernel(L0_attn_x, L0_attn_w_attn, L0_attn_w_proj, L0_ffn_w_gate, L0_ffn_w_up, L0_ffn_w_down):
    x = np.ascontiguousarray(np.asarray(L0_attn_x, dtype=np.float32))
    w_attn = np.ascontiguousarray(np.asarray(L0_attn_w_attn, dtype=np.float32))
    w_proj = np.ascontiguousarray(np.asarray(L0_attn_w_proj, dtype=np.float32))
    w_gate = np.ascontiguousarray(np.asarray(L0_ffn_w_gate, dtype=np.float32))
    w_up = np.ascontiguousarray(np.asarray(L0_ffn_w_up, dtype=np.float32))
    w_down = np.ascontiguousarray(np.asarray(L0_ffn_w_down, dtype=np.float32))
    cos2, sin2, tri, sel, idn = _host_consts()
    nc = build_program()
    in_maps = [{
        "x": x[c], "w_attn": w_attn, "w_proj": w_proj,
        "w_gate": w_gate, "w_up": w_up, "w_down": w_down,
        "cos2": cos2, "sin2": sin2, "tri": tri, "sel": sel, "idn": idn,
    } for c in range(NCORES)]
    res = run_bass_kernel_spmd(nc, in_maps, core_ids=list(range(NCORES)))
    return np.stack([res.results[c]["y"] for c in range(NCORES)], axis=0).astype(np.float32)


# version 40 (final)
# speedup vs baseline: 1.0864x; 1.0864x over previous
import contextlib
import numpy as np
import ml_dtypes
import concourse.bass as bass
import concourse.mybir as mybir
from concourse.alu_op_type import AluOpType as OP
from concourse.bass_utils import run_bass_kernel_spmd

F32 = mybir.dt.float32
BF16 = mybir.dt.bfloat16
AF = mybir.ActivationFunctionType
AX = mybir.AxisListType
bfdt = ml_dtypes.bfloat16

T, C = 4096, 1024
NT, NQC = 32, 8
NH, NKV, HD = 16, 4, 64
QKV, KVD = 1536, 256
I = 2816
NIC = 6
IC_W = [512] * 5 + [256]
IC_OFF = [0, 512, 1024, 1536, 2048, 2560]
NCORES = 8
ENGINES = ["sync", "gpsimd", "tensor", "vector", "scalar"]


class Prog:
    def __init__(self, nc, es):
        self.nc = nc
        self.steps = []
        self.sems = {}
        self.cnt = {}
        for e in ENGINES + ["dsync", "dgp"]:
            self.sems[e] = es.enter_context(nc.semaphore("sem_" + e))
            self.cnt[e] = 0
        self.last_plain = {"sync": None, "gpsimd": None}
        self.last_tp = None

    def step(self, eng, deps, fn):
        self.cnt[eng] += 1
        tok = (eng, self.cnt[eng])
        self.steps.append((eng, [d for d in deps if d is not None], fn, "c", tok))
        return tok

    def dma(self, eng, deps, out, in_, transpose=False):
        deps = list(deps)
        if transpose:
            deps += [t for t in (self.last_plain["sync"], self.last_plain["gpsimd"]) if t]
        elif self.last_tp is not None:
            deps.append(self.last_tp)
        d = {"sync": "dsync", "gpsimd": "dgp"}[eng]
        self.cnt[d] += 16
        dval = self.cnt[d]

        def fn(eh, out=out, in_=in_, transpose=transpose):
            if transpose:
                return eh.dma_start_transpose(out=out, in_=in_)
            return eh.dma_start(out=out, in_=in_)
        self.steps.append((eng, [x for x in deps if x is not None], fn, "d", (d, dval)))
        self.cnt[eng] += 1
        tok = (eng, self.cnt[eng])
        self.steps.append((eng, [(d, dval)], None, "k", tok))
        if transpose:
            self.last_tp = tok
        else:
            self.last_plain[eng] = tok
        return tok

    def dma_batch(self, eng, deps, xfers, transpose=False):
        deps = list(deps)
        if transpose:
            deps += [t for t in (self.last_plain["sync"], self.last_plain["gpsimd"]) if t]
        elif self.last_tp is not None:
            deps.append(self.last_tp)
        d = {"sync": "dsync", "gpsimd": "dgp"}[eng]
        dval = None
        for j, (out, in_) in enumerate(xfers):
            self.cnt[d] += 16
            dval = self.cnt[d]

            def fn(eh, out=out, in_=in_, transpose=transpose):
                if transpose:
                    return eh.dma_start_transpose(out=out, in_=in_)
                return eh.dma_start(out=out, in_=in_)
            self.steps.append((eng, [x for x in (deps if j == 0 else []) if x is not None],
                               fn, "d", (d, dval)))
        self.cnt[eng] += 1
        tok = (eng, self.cnt[eng])
        self.steps.append((eng, [(d, dval)], None, "k", tok))
        if transpose:
            self.last_tp = tok
        else:
            self.last_plain[eng] = tok
        return tok

    def emit(self, block):
        for eng in ENGINES:
            mysteps = [s for s in self.steps if s[0] == eng]
            if not mysteps:
                continue

            def make(mysteps=mysteps, eng=eng):
                def stream(eh):
                    waited = {}
                    for _, deps, fn, kind, tok in mysteps:
                        for (sem, val) in deps:
                            if sem == eng:
                                continue
                            if waited.get(sem, -1) >= val:
                                continue
                            eh.wait_ge(self.sems[sem], val)
                            waited[sem] = val
                        if kind == "k":
                            eh.sem_inc(self.sems[eng], 1)
                        elif kind == "c":
                            fn(eh).then_inc(self.sems[eng], 1)
                        else:
                            fn(eh).then_inc(self.sems[tok[0]], 16)
                    if eng == "sync":
                        if self.cnt["dsync"]:
                            eh.wait_ge(self.sems["dsync"], self.cnt["dsync"])
                        if self.cnt["dgp"]:
                            eh.wait_ge(self.sems["dgp"], self.cnt["dgp"])
                        if self.cnt["vector"]:
                            eh.wait_ge(self.sems["vector"], self.cnt["vector"])
                return stream
            getattr(block, eng)(make())


def bc3(ap2d, nblk, rep):
    return ap2d.unsqueeze(2).broadcast_to([ap2d.shape[0], nblk, rep])


def r16(ap2d):
    return ap2d.rearrange("p (a s) -> p a s", s=16)


OFF_B = 90112
OFF_C = OFF_B + 53248
ARENA_BYTES = OFF_C + 61440


def build_program():
    nc = bass.Bass("TRN2", target_bir_lowering=False)

    x_d = nc.dram_tensor("x", [T, C], F32, kind="ExternalInput")
    wattn_d = nc.dram_tensor("w_attn", [C, QKV], F32, kind="ExternalInput")
    wproj_d = nc.dram_tensor("w_proj", [C, C], F32, kind="ExternalInput")
    wgate_d = nc.dram_tensor("w_gate", [I, C], F32, kind="ExternalInput")
    wup_d = nc.dram_tensor("w_up", [I, C], F32, kind="ExternalInput")
    wdown_d = nc.dram_tensor("w_down", [C, I], F32, kind="ExternalInput")
    cos_d = nc.dram_tensor("cos2", [T, HD], F32, kind="ExternalInput")
    sin_d = nc.dram_tensor("sin2", [T, HD], F32, kind="ExternalInput")
    tri_d = nc.dram_tensor("tri", [128, 128], BF16, kind="ExternalInput")
    sel_d = nc.dram_tensor("sel", [128, 128], BF16, kind="ExternalInput")
    idn_d = nc.dram_tensor("idn", [128, 128], BF16, kind="ExternalInput")
    y_d = nc.dram_tensor("y", [T, C], F32, kind="ExternalOutput")
    xn_d = nc.dram_tensor("xn_s", [T, C], BF16)
    xq_d = nc.dram_tensor("xq_s", [T, C], BF16)
    q_d = nc.dram_tensor("q_s", [T, C], BF16)
    k_d = nc.dram_tensor("k_s", [T, 2 * KVD], BF16)
    wgq_d = nc.dram_tensor("wgq_s", [I, C], BF16)
    wuq_d = nc.dram_tensor("wuq_s", [I, C], BF16)
    wdq_d = nc.dram_tensor("wdq_s", [C, I], BF16)
    yt_d = nc.dram_tensor("yt_s", [C, T], BF16)
    resid_d = nc.dram_tensor("resid_s", [T, C], F32)
    l_d = nc.dram_tensor("l_s", [4, 512], F32)
    linv_d = nc.dram_tensor("linv_s", [4, 512], F32)

    es = contextlib.ExitStack()
    P = Prog(nc, es)
    sb = lambda n, sh, dt: es.enter_context(nc.sbuf_tensor(n, sh, dt))[:]
    arena_h = es.enter_context(nc.sbuf_tensor("arena", [128, ARENA_BYTES // 2], BF16))

    def carve(off_b, nelem, dt, parts=128):
        assert off_b % 64 == 0
        nb = nelem * (4 if dt == F32 else 2)
        v = arena_h[0:parts, off_b // 2:(off_b + nb) // 2]
        if dt != BF16:
            v = v.bitcast(dt)
        return v

    psA = es.enter_context(nc.psum_tensor("psA", [128, 2048], F32))
    psB = es.enter_context(nc.psum_tensor("psB", [128, 2048], F32))

    tri = sb("tri_s", [128, 128], BF16)
    sel = sb("sel_s", [128, 128], BF16)
    idn = sb("idn_s", [128, 128], BF16)
    L128 = sb("L128_s", [128, 512], BF16)

    t_tri = P.dma("sync", [], tri, tri_d[:, :])
    t_sel = P.dma("sync", [], sel, sel_d[:, :])
    t_idn = P.dma("sync", [], idn, idn_d[:, :])
    t_L0 = P.step("gpsimd", [], lambda g: g.memset(L128, 0.0))

    xt = [carve(4096 * b, C, F32) for b in range(3)]
    xn = [carve(12288 + 4096 * b, C, F32) for b in range(4)]
    qt_ = [carve(28672 + 4096 * b, C, F32) for b in range(2)]
    qu_ = [carve(36864 + 4096 * b, C, F32) for b in range(2)]
    qU4 = [carve(45056 + 4096 * b, C, F32) for b in range(2)]
    qP = [carve(53248 + 2048 * b, C, BF16) for b in range(2)]
    qQ = [carve(57344 + 2048 * b, C, BF16) for b in range(2)]
    qR1 = [carve(61440 + 2048 * b, C, BF16) for b in range(2)]
    qR2 = [carve(65536 + 2048 * b, C, BF16) for b in range(2)]
    qXQ = [carve(69632 + 2048 * b, C, BF16) for b in range(2)]
    xnTg = [carve(0, 8 * 512, BF16), carve(8192, 8 * 512, BF16)]
    wa = carve(16384, 8 * QKV, BF16)
    v65 = carve(40960, NT * 260, BF16)
    wp = carve(0, 8 * C, BF16)
    xb3 = [carve(16384 + 4096 * b, C, F32) for b in range(3)]
    rs3 = [carve(28672 + 4096 * b, C, F32) for b in range(3)]
    wgT = carve(0, 8 * I, BF16)
    wuT = carve(45056, 8 * I, BF16)
    wtb2 = [carve(57600 + 4096 * b, C, F32) for b in range(3)]
    qt2 = [carve(69888 + 4096 * b, C, F32) for b in range(2)]
    qu2 = [carve(78080 + 4096 * b, C, F32) for b in range(2)]

    B = OFF_B
    cosb = carve(B + 0, NT * HD, F32)
    sinb = carve(B + 8192, NT * HD, F32)
    m1 = carve(B + 16384, C + KVD, F32)
    m2 = carve(B + 21504, C + KVD, F32)
    rot = [carve(B + 26624, C + KVD, F32), carve(B + 31744, C + KVD, F32)]
    sqq = m1
    qn = [carve(B + 36864, C, BF16), carve(B + 38912, C, BF16)]
    kn = [carve(B + 40960, KVD, BF16), carve(B + 41472, KVD, BF16)]
    qTg = [carve(B + 0, 2 * T, BF16), carve(B + 16384, 2 * T, BF16)]
    kTd = carve(B + 32768, T, BF16)
    PT = [carve(B + 40960, 1024, BF16), carve(B + 43008, 1024, BF16), carve(B + 45056, 1024, BF16)]
    ybuf = carve(B + 51200, 512, BF16)
    wdT = carve(B + 0, 22 * C, BF16)
    xqT = carve(B + 45056, 8 * 512, BF16)

    Cb = OFF_C
    yTq = carve(Cb + 0, 8 * 512, BF16)
    yTq2 = carve(Cb + 28672, 8 * 512, BF16)
    Osb = carve(Cb + 8192, 1024, F32, parts=65)
    L2 = [carve(Cb + 12288, 512, F32, parts=2), carve(Cb + 14336, 512, F32, parts=2)]
    Linv = [carve(Cb + 16384, 512, F32, parts=2), carve(Cb + 18432, 512, F32, parts=2)]
    bcb = [[carve(Cb + 20480, 512, F32, parts=64), carve(Cb + 22528, 512, F32, parts=64)],
           [carve(Cb + 24576, 512, F32, parts=64), carve(Cb + 26624, 512, F32, parts=64)]]
    hc = [carve(Cb + 2048 * b, 512, F32) for b in range(2)]
    tc = [carve(Cb + 4096 + 2048 * b, 512, F32) for b in range(2)]
    u2c = [carve(Cb + 8192 + 2048 * b, 512, F32) for b in range(2)]
    u4c = [carve(Cb + 12288 + 2048 * b, 512, F32) for b in range(2)]
    Pc = [carve(Cb + 16384 + 1024 * b, 512, BF16) for b in range(2)]
    Qc = [carve(Cb + 18432 + 1024 * b, 512, BF16) for b in range(2)]
    R1c = [carve(Cb + 20480 + 1024 * b, 512, BF16) for b in range(2)]
    R2c = [carve(Cb + 22528 + 1024 * b, 512, BF16) for b in range(2)]
    XD = 10
    xqc = [carve(Cb + 24576 + 1024 * j, 512, BF16) for j in range(XD)]
    sg = [carve(Cb + 34816 + 2048 * b, 512, F32) for b in range(2)]
    hqT2 = [carve(Cb + 38912 + 1024 * b, 512, BF16) for b in range(2)]
    rsb2 = [carve(Cb + 40960 + 4096 * b, C, F32) for b in range(3)]
    xqT2 = carve(Cb + 53248, 8 * 512, BF16)
    qU42 = [carve(Cb + 28672 + 4096 * b, C, F32) for b in range(2)]
    qP2 = [carve(Cb + 36864 + 2048 * b, C, BF16) for b in range(2)]
    qQ2 = [carve(Cb + 40960 + 2048 * b, C, BF16) for b in range(2)]
    qR12 = [carve(Cb + 45056 + 2048 * b, C, BF16) for b in range(2)]
    qR22 = [carve(Cb + 49152 + 2048 * b, C, BF16) for b in range(2)]
    qXQ2 = [carve(Cb + 53248 + 2048 * b, C, BF16) for b in range(2)]

    st1 = {nm: [sb(f"p1_{nm}{b}", [128, 1], F32) for b in range(2)]
           for nm in ["sum", "ssq", "negm", "msq", "nmse", "lnv", "rstd"]}
    qbm = [sb(f"qbm{b}", [128, 64], F32) for b in range(2)]
    qsc = [sb(f"qsc{b}", [128, 64], F32) for b in range(3)]
    qs_ = [sb(f"qs{b}", [128, 64], F32) for b in range(3)]
    qln = [sb(f"qln{b}", [128, 64], F32) for b in range(3)]
    qr_ = [sb(f"qr{b}", [128, 64], F32) for b in range(3)]
    junkB = carve(OFF_B + 0, C, F32)
    ms2 = [sb(f"p2ms{b}", [128, 20], F32) for b in range(2)]
    lnm = [sb(f"p2lm{b}", [128, 20], F32) for b in range(2)]
    rsd = [sb(f"p2rs{b}", [128, 20], F32) for b in range(2)]
    q4bm = [sb(f"q4bm{b}", [128, 32], F32) for b in range(2)]
    q4sc = [sb(f"q4sc{b}", [128, 32], F32) for b in range(2)]
    q4s = [sb(f"q4s{b}", [128, 32], F32) for b in range(2)]
    q4ln = [sb(f"q4ln{b}", [128, 32], F32) for b in range(2)]
    q4r = [sb(f"q4r{b}", [128, 32], F32) for b in range(2)]

    def quant(bufs, b, w, src, deps, dep_prev_z, dep_prev_out, vec_scale=False):
        nb = w // 16
        tb, ub = bufs["t"][b][:, 0:w], bufs["u"][b][:, 0:w]
        Pb, Qb = bufs["P"][b][:, 0:w], bufs["Q"][b][:, 0:w]
        xq = bufs["xq"][b][:, 0:w]
        bm, scb, sqb, lnb, rqb = (bufs[k][b][:, 0:nb] for k in ["bm", "sc", "s", "ln", "r"])
        pz = [dep_prev_z] if dep_prev_z else []
        po = [dep_prev_out] if dep_prev_out else []

        t_bm = P.step("vector", deps + pz, lambda v: v.tensor_reduce(
            out=bm, in_=r16(src), axis=AX.X, op=OP.max, apply_absolute_value=True))
        if vec_scale:
            def vsc(v):
                v.drain()
                v.tensor_scalar(out=sqb, in0=bm, scalar1=1.0 / 6.0, scalar2=1e-12, op0=OP.mult, op1=OP.max)
                v.drain()
                return v.reciprocal(out=rqb, in_=sqb)
            t_qs = P.step("vector", [t_bm], vsc)
        else:
            t_qs = P.step("scalar", [t_bm], lambda sc: (
                sc.activation(out=scb, in_=bm, func=AF.Relu, scale=1.0 / 6.0),
                sc.activation(out=sqb, in_=scb, func=AF.Copy, bias=1e-12),
                sc.activation(out=lnb, in_=sqb, func=AF.Ln),
                sc.activation(out=rqb, in_=lnb, func=AF.Exp, scale=-1.0))[-1])
        if "R1" in bufs:
            u4b = bufs["u4"][b][:, 0:w]
            R1, R2 = bufs["R1"][b][:, 0:w], bufs["R2"][b][:, 0:w]
            t_t = P.step("vector", [t_qs], lambda v: (
                v.tensor_tensor(out=r16(tb), in0=r16(src), in1=bc3(rqb, nb, 16), op=OP.mult),
                v.tensor_scalar(out=ub, in0=tb, scalar1=2.0, scalar2=-2.0, op0=OP.min, op1=OP.max),
                v.tensor_scalar(out=u4b, in0=tb, scalar1=4.0, scalar2=-4.0, op0=OP.min, op1=OP.max))[-1])
            t_cp = P.step("scalar", [t_t] + po, lambda sc: (
                sc.activation(out=Pb, in_=ub, func=AF.Copy, bias=96.0),
                sc.activation(out=Qb, in_=ub, func=AF.Copy, scale=-1.0, bias=-192.0),
                sc.activation(out=R1, in_=u4b, func=AF.Copy, bias=192.0),
                sc.activation(out=R2, in_=u4b, func=AF.Copy, scale=-1.0, bias=-416.0),
                sc.activation(out=xq, in_=tb, func=AF.Copy, bias=320.0))[-1])
            t_z = P.step("vector", [t_cp], lambda v: (
                v.tensor_tensor(out=Pb, in0=Pb, in1=Qb, op=OP.add),
                v.tensor_tensor(out=R1, in0=R1, in1=R2, op=OP.add),
                v.tensor_tensor(out=R1, in0=R1, in1=xq, op=OP.add),
                v.tensor_tensor(out=Pb, in0=Pb, in1=R1, op=OP.add),
                v.tensor_tensor(out=r16(xq), in0=r16(Pb), in1=bc3(sqb, nb, 16), op=OP.mult))[-1])
            return t_z
        Rb = xq
        t_t = P.step("vector", [t_qs], lambda v: (
            v.tensor_tensor(out=r16(tb), in0=r16(src), in1=bc3(rqb, nb, 16), op=OP.mult),
            v.tensor_scalar(out=ub, in0=tb, scalar1=2.0, scalar2=-2.0, op0=OP.min, op1=OP.max))[-1])
        t_ca = P.step("scalar", [t_t], lambda sc: (
            sc.activation(out=Pb, in_=ub, func=AF.Copy, bias=96.0),
            sc.activation(out=Qb, in_=ub, func=AF.Copy, scale=-1.0, bias=-192.0))[-1])
        t_v1 = P.step("vector", [t_ca], lambda v: (
            v.tensor_tensor(out=Pb, in0=Pb, in1=Qb, op=OP.add),
            v.tensor_scalar(out=ub, in0=tb, scalar1=4.0, scalar2=-4.0, op0=OP.min, op1=OP.max))[-1])
        t_cb = P.step("scalar", [t_v1], lambda sc: (
            sc.activation(out=Qb, in_=ub, func=AF.Copy, bias=192.0),
            sc.activation(out=Rb, in_=ub, func=AF.Copy, scale=-1.0, bias=-416.0))[-1])
        t_v2 = P.step("vector", [t_cb], lambda v: v.tensor_tensor(out=Qb, in0=Qb, in1=Rb, op=OP.add))
        t_cc = P.step("scalar", [t_v2] + po, lambda sc: sc.activation(out=Rb, in_=tb, func=AF.Copy, bias=320.0))
        t_z = P.step("vector", [t_cc], lambda v: (
            v.tensor_tensor(out=Qb, in0=Qb, in1=Rb, op=OP.add),
            v.tensor_tensor(out=Pb, in0=Pb, in1=Qb, op=OP.add),
            v.tensor_tensor(out=r16(xq), in0=r16(Pb), in1=bc3(sqb, nb, 16), op=OP.mult))[-1])
        return t_z

    qb1 = {"t": qt_, "u": qu_, "u4": qU4, "P": qP, "Q": qQ, "R1": qR1, "R2": qR2, "xq": qXQ,
           "bm": qbm, "sc": qsc, "s": qs_, "ln": qln, "r": qr_}

    NSTG = 11
    tokt = {}

    def p1_stage(i, k):
        b2, b3, b4 = i % 2, i % 3, i % 4
        g = lambda ii, kk: tokt.get((ii, kk)) if ii >= 0 else None
        if k == 0:
            return P.dma("sync", [g(i - 3, 3)], xt[b3], x_d[128 * i:128 * (i + 1), :])
        if k == 1:
            return P.step("vector", [tokt[(i, 0)], g(i - 2, 2)], lambda v: v.tensor_reduce(
                out=st1["sum"][b2], in_=xt[b3], axis=AX.X, op=OP.add))
        if k == 2:
            return P.step("scalar", [tokt[(i, 0)], tokt[(i, 1)], g(i - 2, 3)], lambda sc: (
                sc.activation(out=st1["negm"][b2], in_=st1["sum"][b2], func=AF.Copy, scale=-1.0 / C),
                sc.activation(out=st1["msq"][b2], in_=st1["negm"][b2], func=AF.Square),
                sc.activation(out=st1["nmse"][b2], in_=st1["msq"][b2], func=AF.Copy, scale=-1.0, bias=1e-5),
                sc.activation(out=junkB, in_=xt[b3], func=AF.Square, accum_out=st1["ssq"][b2]),
                sc.activation(out=st1["lnv"][b2], in_=st1["ssq"][b2], func=AF.Ln, scale=1.0 / C, bias=st1["nmse"][b2]),
                sc.activation(out=st1["rstd"][b2], in_=st1["lnv"][b2], func=AF.Exp, scale=-0.5))[-1])
        if k == 3:
            return P.step("vector", [tokt[(i, 2)], g(i - 4, 5), g(i - 4, 7)], lambda v: v.tensor_scalar(
                out=xn[b4], in0=xt[b3], scalar1=st1["negm"][b2], scalar2=st1["rstd"][b2],
                op0=OP.add, op1=OP.mult))
        if k == 4:
            return P.step("vector", [tokt[(i, 3)], g(i - 2, 6)], lambda v: v.tensor_reduce(
                out=qbm[b2], in_=r16(xn[b4]), axis=AX.X, op=OP.max, apply_absolute_value=True))
        if k == 5:
            return P.dma("gpsimd", [tokt[(i, 3)]], xn_d[128 * i:128 * (i + 1), :], xn[b4])
        if k == 6:
            return P.step("scalar", [tokt[(i, 4)], g(i - 3, 9)], lambda sc: (
                sc.activation(out=qsc[b3], in_=qbm[b2], func=AF.Relu, scale=1.0 / 6.0),
                sc.activation(out=qs_[b3], in_=qsc[b3], func=AF.Copy, bias=1e-12),
                sc.activation(out=qln[b3], in_=qs_[b3], func=AF.Ln),
                sc.activation(out=qr_[b3], in_=qln[b3], func=AF.Exp, scale=-1.0))[-1])
        if k == 7:
            return P.step("vector", [tokt[(i, 6)], g(i - 2, 8)], lambda v: (
                v.tensor_tensor(out=r16(qt_[b2]), in0=r16(xn[b4]), in1=bc3(qr_[b3], 64, 16), op=OP.mult),
                v.tensor_scalar(out=qu_[b2], in0=qt_[b2], scalar1=2.0, scalar2=-2.0, op0=OP.min, op1=OP.max),
                v.tensor_scalar(out=qU4[b2], in0=qt_[b2], scalar1=4.0, scalar2=-4.0, op0=OP.min, op1=OP.max))[-1])
        if k == 8:
            return P.step("scalar", [tokt[(i, 7)], g(i - 2, 9), g(i - 2, 10)], lambda sc: (
                sc.activation(out=qP[b2], in_=qu_[b2], func=AF.Copy, bias=96.0),
                sc.activation(out=qQ[b2], in_=qu_[b2], func=AF.Copy, scale=-1.0, bias=-192.0),
                sc.activation(out=qR1[b2], in_=qU4[b2], func=AF.Copy, bias=192.0),
                sc.activation(out=qR2[b2], in_=qU4[b2], func=AF.Copy, scale=-1.0, bias=-416.0),
                sc.activation(out=qXQ[b2], in_=qt_[b2], func=AF.Copy, bias=320.0))[-1])
        if k == 9:
            return P.step("vector", [tokt[(i, 8)]], lambda v: (
                v.tensor_tensor(out=qP[b2], in0=qP[b2], in1=qQ[b2], op=OP.add),
                v.tensor_tensor(out=qR1[b2], in0=qR1[b2], in1=qR2[b2], op=OP.add),
                v.tensor_tensor(out=qR1[b2], in0=qR1[b2], in1=qXQ[b2], op=OP.add),
                v.tensor_tensor(out=qP[b2], in0=qP[b2], in1=qR1[b2], op=OP.add),
                v.tensor_tensor(out=r16(qXQ[b2]), in0=r16(qP[b2]), in1=bc3(qs_[b3], 64, 16), op=OP.mult))[-1])
        if k == 10:
            return P.dma("sync", [tokt[(i, 9)]], xq_d[128 * i:128 * (i + 1), :], qXQ[b2])

    for tau in range(NT + NSTG - 1):
        for k in range(NSTG - 1, -1, -1):
            i = tau - k
            if 0 <= i < NT:
                tokt[(i, k)] = p1_stage(i, k)
    tok_xnv = [tokt[(i, 3)] for i in range(NT)]
    tok_xndma = [tokt[(i, 5)] for i in range(NT)]
    tok_xqdma = [tokt[(i, 10)] for i in range(NT)]
    tok_ln1 = [tokt[(i, 2)] for i in range(NT)]
    tok_z1 = [tokt[(i, 9)] for i in range(NT)]

    wtiles = ([("g", (i, 0), C) for i in range(22)] + [("u", (i, 0), C) for i in range(22)]
              + [("d", (i, cc), (1024 if cc < 2 else 768)) for i in range(8) for cc in range(3)])
    NWS = 7
    tokw = {}
    p1_last = tokt[(NT - 1, 10)]

    def p1b_stage(n, k):
        kind, (i, cc), w = wtiles[n]
        b2, b3 = n % 2, n % 3
        nb = w // 16
        srcd = {"g": wgate_d, "u": wup_d, "d": wdown_d}[kind]
        dstd = {"g": wgq_d, "u": wuq_d, "d": wdq_d}[kind]
        sl_src = srcd[128 * i:128 * (i + 1), 1024 * cc:1024 * cc + w]
        sl_dst = dstd[128 * i:128 * (i + 1), 1024 * cc:1024 * cc + w]

        def g(nn, kk):
            if nn >= 0:
                return tokw.get((nn, kk))
            return p1_last
        if k == 0:
            return P.dma("sync", [g(n - 3, 3)], wtb2[b3][:, 0:w], sl_src)
        if k == 1:
            return P.step("vector", [tokw[(n, 0)], g(n - 2, 2)], lambda v: v.tensor_reduce(
                out=qbm[b2][:, 0:nb], in_=r16(wtb2[b3][:, 0:w]), axis=AX.X, op=OP.max,
                apply_absolute_value=True))
        if k == 2:
            return P.step("scalar", [tokw[(n, 1)], g(n - 3, 5)], lambda sc: (
                sc.activation(out=qsc[b3][:, 0:nb], in_=qbm[b2][:, 0:nb], func=AF.Relu, scale=1.0 / 6.0),
                sc.activation(out=qs_[b3][:, 0:nb], in_=qsc[b3][:, 0:nb], func=AF.Copy, bias=1e-12),
                sc.activation(out=qln[b3][:, 0:nb], in_=qs_[b3][:, 0:nb], func=AF.Ln),
                sc.activation(out=qr_[b3][:, 0:nb], in_=qln[b3][:, 0:nb], func=AF.Exp, scale=-1.0))[-1])
        if k == 3:
            return P.step("vector", [tokw[(n, 2)], g(n - 2, 4)], lambda v: (
                v.tensor_tensor(out=r16(qt2[b2][:, 0:w]), in0=r16(wtb2[b3][:, 0:w]),
                                in1=bc3(qr_[b3][:, 0:nb], nb, 16), op=OP.mult),
                v.tensor_scalar(out=qu2[b2][:, 0:w], in0=qt2[b2][:, 0:w], scalar1=2.0, scalar2=-2.0,
                                op0=OP.min, op1=OP.max),
                v.tensor_scalar(out=qU42[b2][:, 0:w], in0=qt2[b2][:, 0:w], scalar1=4.0, scalar2=-4.0,
                                op0=OP.min, op1=OP.max))[-1])
        if k == 4:
            return P.step("scalar", [tokw[(n, 3)], g(n - 2, 5), g(n - 2, 6)], lambda sc: (
                sc.activation(out=qP2[b2][:, 0:w], in_=qu2[b2][:, 0:w], func=AF.Copy, bias=96.0),
                sc.activation(out=qQ2[b2][:, 0:w], in_=qu2[b2][:, 0:w], func=AF.Copy, scale=-1.0, bias=-192.0),
                sc.activation(out=qR12[b2][:, 0:w], in_=qU42[b2][:, 0:w], func=AF.Copy, bias=192.0),
                sc.activation(out=qR22[b2][:, 0:w], in_=qU42[b2][:, 0:w], func=AF.Copy, scale=-1.0, bias=-416.0),
                sc.activation(out=qXQ2[b2][:, 0:w], in_=qt2[b2][:, 0:w], func=AF.Copy, bias=320.0))[-1])
        if k == 5:
            return P.step("vector", [tokw[(n, 4)]], lambda v: (
                v.tensor_tensor(out=qP2[b2][:, 0:w], in0=qP2[b2][:, 0:w], in1=qQ2[b2][:, 0:w], op=OP.add),
                v.tensor_tensor(out=qR12[b2][:, 0:w], in0=qR12[b2][:, 0:w], in1=qR22[b2][:, 0:w], op=OP.add),
                v.tensor_tensor(out=qR12[b2][:, 0:w], in0=qR12[b2][:, 0:w], in1=qXQ2[b2][:, 0:w], op=OP.add),
                v.tensor_tensor(out=qP2[b2][:, 0:w], in0=qP2[b2][:, 0:w], in1=qR12[b2][:, 0:w], op=OP.add),
                v.tensor_tensor(out=r16(qXQ2[b2][:, 0:w]), in0=r16(qP2[b2][:, 0:w]),
                                in1=bc3(qs_[b3][:, 0:nb], nb, 16), op=OP.mult))[-1])
        if k == 6:
            return P.dma("gpsimd", [tokw[(n, 5)]], sl_dst, qXQ2[b2][:, 0:w])

    p1b_entries = []
    for tau in range(len(wtiles) + NWS - 1):
        for k in range(NWS - 1, -1, -1):
            n = tau - k
            if 0 <= n < len(wtiles):
                p1b_entries.append((n, k))

    def emit_p1b(cnt):
        while cnt > 0 and p1b_entries:
            n, k = p1b_entries.pop(0)
            tokw[(n, k)] = p1b_stage(n, k)
            cnt -= 1

    emit_p1b(len(p1b_entries))
    tok_wq_last = tokw[(len(wtiles) - 1, 6)]

    pq = [psA[:, 0:QKV], psB[:, 0:QKV]]
    tok_cos = P.dma("sync", [tok_ln1[NT - 1]], cosb.rearrange("p (n f) -> p n f", f=HD), cos_d[:, :].rearrange("(n p) f -> p n f", p=128))
    tok_sin = P.dma("sync", [tok_ln1[NT - 1]], sinb.rearrange("p (n f) -> p n f", f=HD), sin_d[:, :].rearrange("(n p) f -> p n f", p=128))
    tok_p1q = [tok_z1[NT - 1]]
    tok_wa = [P.dma("gpsimd", tok_p1q, wa[:, QKV * cb:QKV * (cb + 1)],
                    wattn_d[128 * cb:128 * (cb + 1), :]) for cb in range(8)]
    t_v65i = P.step("gpsimd", tok_p1q, lambda g: g.memset(v65, 1.0))

    tok_xnTg = [None] * 8
    tok_mm2 = [None] * NT
    tok_rope = [None] * NT
    tok_vcp = [None] * NT
    tok_qnv = [None] * NT
    tok_qdma = [None] * NT
    tok_kdma = [None] * NT
    tok_sq2 = [None] * NT
    tok_ms = [None] * NT
    tok_p1x = [tokt[(NT - 1, 3)], tokt[(28, 7)]]
    for i in range(NT):
        b = i % 2
        pr = i - 2
        grp, ti = divmod(i, 4)
        if ti == 0:
            dep = [tok_xndma[4 * grp + k] for k in range(4)] + tok_p1x
            if grp >= 2:
                dep.append(tok_mm2[4 * (grp - 2) + 3])
            tok_xnTg[grp] = [P.dma("sync", dep, xnTg[grp % 2][:, 512 * cb:512 * (cb + 1)],
                                   xn_d[512 * grp:512 * (grp + 1), 128 * cb:128 * (cb + 1)],
                                   transpose=True) for cb in range(8)]

        def mmfn(t, b=b, ti=ti, gb=grp % 2):
            for nch in range(3):
                for cb in range(8):
                    ins = t.matmul(out=pq[b][:, 512 * nch:512 * (nch + 1)],
                                   lhsT=xnTg[gb][:, 512 * cb + 128 * ti:512 * cb + 128 * (ti + 1)],
                                   rhs=wa[:, QKV * cb + 512 * nch:QKV * cb + 512 * (nch + 1)],
                                   start=(cb == 0), stop=(cb == 7))
            return ins
        depm = tok_xnTg[grp] + tok_wa
        if pr >= 0:
            depm = depm + [tok_rope[pr], tok_vcp[pr]]
        tok_mm2[i] = P.step("tensor", depm, mmfn)

        def ropefn(v, b=b, i=i):
            co = HD * i
            pst = list(pq[b].ap[0])
            pofs = pq[b].offset
            cview = cosb[:, co:co + HD]
            sview = sinb[:, co:co + HD]
            v.tensor_tensor(out=m1[:, 0:C].rearrange("p (h d) -> p h d", d=HD),
                            in0=pq[b][:, 0:C].rearrange("p (h d) -> p h d", d=HD),
                            in1=cview.unsqueeze(1).broadcast_to([128, NH, HD]), op=OP.mult)
            v.tensor_tensor(out=m2[:, 0:C].rearrange("p (h d) -> p h d", d=HD),
                            in0=bass.AP(pq[b].tensor, pofs + 1, [pst, [HD, NH], [2, HD // 2], [-1, 2]]),
                            in1=sview.unsqueeze(1).broadcast_to([128, NH, HD]), op=OP.mult)
            v.tensor_tensor(out=m1[:, C:C + KVD].rearrange("p (h d) -> p h d", d=HD),
                            in0=pq[b][:, C:C + KVD].rearrange("p (h d) -> p h d", d=HD),
                            in1=cview.unsqueeze(1).broadcast_to([128, NKV, HD]), op=OP.mult)
            v.tensor_tensor(out=m2[:, C:C + KVD].rearrange("p (h d) -> p h d", d=HD),
                            in0=bass.AP(pq[b].tensor, pofs + C + 1, [pst, [HD, NKV], [2, HD // 2], [-1, 2]]),
                            in1=sview.unsqueeze(1).broadcast_to([128, NKV, HD]), op=OP.mult)
            return v.tensor_tensor(out=rot[b], in0=m1, in1=m2, op=OP.add)
        rdeps = [tok_mm2[i], tok_cos, tok_sin]
        if i >= 1:
            rdeps.append(tok_sq2[i - 1])
        if pr >= 0:
            rdeps.append(tok_qnv[pr])
        tok_rope[i] = P.step("vector", rdeps, ropefn)

        def vcpfn(v, b=b, i=i):
            vdst = v65[:, 260 * i:260 * (i + 1)].rearrange("p (g u) -> p g u", u=65)[:, :, 0:64]
            return v.tensor_scalar(out=vdst, in0=pq[b][:, C + KVD:QKV].rearrange("p (g d) -> p g d", d=64),
                                   scalar1=1.0, scalar2=None, op0=OP.mult)
        tok_vcp[i] = P.step("vector", [tok_mm2[i], t_v65i], vcpfn)

        tok_sq2[i] = P.step("scalar", [tok_rope[i]] + ([tok_ms[i - 1]] if i >= 1 else []),
                            lambda sc, b=b: sc.activation(out=sqq, in_=rot[b], func=AF.Square))
        tok_ms[i] = P.step("vector", [tok_sq2[i]], lambda v, b=b: v.tensor_reduce(
            out=ms2[b], in_=sqq.rearrange("p (h d) -> p h d", d=HD), axis=AX.X, op=OP.add))
        t_rms = P.step("scalar", [tok_ms[i]], lambda sc, b=b: (
            sc.activation(out=lnm[b], in_=ms2[b], func=AF.Copy, scale=1.0 / HD, bias=1e-6),
            sc.activation(out=lnm[b], in_=lnm[b], func=AF.Ln),
            sc.activation(out=rsd[b], in_=lnm[b], func=AF.Exp, scale=-0.5))[-1])

        def qnfn(v, b=b):
            v.tensor_tensor(out=qn[b].rearrange("p (h d) -> p h d", d=HD),
                            in0=rot[b][:, 0:C].rearrange("p (h d) -> p h d", d=HD),
                            in1=bc3(rsd[b][:, 0:NH], NH, HD), op=OP.mult)
            return v.tensor_tensor(out=kn[b].rearrange("p (h d) -> p h d", d=HD),
                                   in0=rot[b][:, C:C + KVD].rearrange("p (h d) -> p h d", d=HD),
                                   in1=bc3(rsd[b][:, NH:NH + NKV], NKV, HD), op=OP.mult)
        tok_qnv[i] = P.step("vector", [t_rms] + ([tok_qdma[pr], tok_kdma[pr]] if pr >= 0 else []), qnfn)
        tok_qdma[i] = P.dma("sync", [tok_qnv[i]], q_d[128 * i:128 * (i + 1), :], qn[b])
        kd3 = k_d[128 * i:128 * (i + 1), :].rearrange("t (g dd) -> t g dd", dd=128)
        ks3 = kn[b].rearrange("t (g d) -> t g d", d=64)
        P.dma("sync", [tok_qnv[i]], kd3[:, :, 0:64], ks3)
        tok_kdma[i] = P.dma("sync", [tok_qnv[i]], kd3[:, :, 64:128], ks3)

    Spp = [psA[:, 0:1024], psA[:, 1024:2048], psB[:, 1024:2048]]
    O_ps = psB[:, 0:1024]

    n_it = 0
    tok_S = {}
    tok_exp = {}
    tok_avp = {}
    tok_oev = None
    tok_yt = None
    tok_hl_slot = [None, None]
    tok_ytd = [[None] * NQC for _ in range(8)]
    tok_ytd_prev = None
    tok_bcb = [None, None]
    pend_av = None
    lslot = 0

    for g in range(4):
        gb = g % 2
        depq = [tok_qdma[NT - 1]]
        if g >= 2:
            depq.append(tok_S[("glast", g - 2)])
        tq0 = P.dma("sync", depq, qTg[gb][:, 0:T], q_d[:, 256 * g:256 * g + 128], transpose=True)
        tq1 = P.dma("sync", depq, qTg[gb][:, T:2 * T], q_d[:, 256 * g + 128:256 * g + 256], transpose=True)
        depk = [tok_kdma[NT - 1]]
        if g >= 1:
            depk.append(tok_S[("glast", g - 1)])
        tk0 = P.dma("sync", depk, kTd, k_d[:, 128 * g:128 * (g + 1)], transpose=True)
        gload = [tq0, tq1, tk0]

        for j in range(2):
            pblk = T * j
            for qc in range(NQC):
                nkt = 4 * qc + 4
                for kt in range(nkt):
                    n = n_it
                    n_it += 1
                    sbi = n % 3
                    diag = kt >= 4 * qc
                    off = 128 * (kt - 4 * qc) if diag else 0

                    def Sfn(t, gb=gb, sbi=sbi, pblk=pblk, qc=qc, kt=kt, off=off, diag=diag):
                        t.matmul(out=Spp[sbi][:, off:512],
                                 lhsT=kTd[0:64, 128 * kt:128 * (kt + 1)],
                                 rhs=qTg[gb][0:64, pblk + 512 * qc + off:pblk + 512 * (qc + 1)],
                                 start=True, stop=not diag, tile_position=(0, 0))
                        ins = t.matmul(out=Spp[sbi][:, 512 + off:1024],
                                       lhsT=kTd[64:128, 128 * kt:128 * (kt + 1)],
                                       rhs=qTg[gb][64:128, pblk + 512 * qc + off:pblk + 512 * (qc + 1)],
                                       start=True, stop=not diag, tile_position=(64, 0))
                        if diag:
                            t.matmul(out=Spp[sbi][:, off:off + 128], lhsT=sel, rhs=tri,
                                     start=False, stop=True, skip_group_check=True)
                            ins = t.matmul(out=Spp[sbi][:, 512 + off:512 + off + 128], lhsT=sel, rhs=tri,
                                           start=False, stop=True, skip_group_check=True)
                        return ins
                    depS = gload + [t_sel, t_tri] + ([tok_exp[n - 3]] if n >= 3 else [])
                    if kt == 0 and tok_oev is not None:
                        depS = depS + [tok_oev]
                    tok_S[n] = P.step("tensor", depS, Sfn)

                    def expfn(sc, sbi=sbi, off=off):
                        pv = PT[sbi].rearrange("p (h w) -> p h w", h=2)[:, :, off:512]
                        sv = Spp[sbi].rearrange("p (h w) -> p h w", h=2)[:, :, off:512]
                        return sc.activation(out=pv, in_=sv, func=AF.Exp, scale=0.125)
                    depE = [tok_S[n]] + ([tok_avp[n - 3]] if n >= 3 else [])
                    tok_exp[n] = P.step("scalar", depE, expfn)

                    dep_av = [tok_exp[n], t_v65i]
                    if kt == 0 and tok_oev is not None:
                        dep_av = dep_av + [tok_oev]

                    def avfn(t, sbi=sbi, kt=kt, g=g, off=off, nkt=nkt):
                        for r in range(2):
                            ins = t.matmul(out=O_ps[0:65, 512 * r + off:512 * (r + 1)],
                                           lhsT=v65[:, 260 * kt + 65 * g:260 * kt + 65 * (g + 1)],
                                           rhs=PT[sbi][:, 512 * r + off:512 * (r + 1)],
                                           start=(kt == 0), stop=(kt == nkt - 1))
                        return ins
                    if pend_av is not None:
                        pn, pfn, pdep = pend_av
                        tok_avp[pn] = P.step("tensor", pdep, pfn)
                    pend_av = (n, avfn, dep_av)

                pn, pfn, pdep = pend_av
                tok_avp[pn] = P.step("tensor", pdep, pfn)
                pend_av = None

                tok_oev = P.step("vector", [tok_avp[n_it - 1]] + ([tok_yt] if tok_yt else []),
                                 lambda v: v.tensor_copy(out=Osb, in_=O_ps[0:65, :]))
                sl = lslot % 2
                lslot += 1
                lsrc = Osb[64:65, :].rearrange("p (a b) -> p a b", b=512)
                t_lo = P.dma("gpsimd", [tok_oev] + ([tok_hl_slot[sl]] if tok_hl_slot[sl] else []),
                             l_d[2 * sl:2 * (sl + 1), :], lsrc)
                t_li = P.dma("gpsimd", [t_lo], L2[sl], l_d[2 * sl:2 * (sl + 1), :])
                t_rc = P.step("vector", [t_li], lambda v, sl=sl: v.reciprocal(out=Linv[sl], in_=L2[sl]))
                tok_hl_slot[sl] = t_rc
                t_lo2 = P.dma("gpsimd", [t_rc], linv_d[2 * sl:2 * (sl + 1), :], Linv[sl])
                bsl = lslot % 2
                bdep = [t_lo2] + ([tok_bcb[bsl]] if tok_bcb[bsl] else [])
                bc0src = bass.AP(linv_d, 2 * sl * 512, [[0, 64], [1, 512]])
                bc1src = bass.AP(linv_d, (2 * sl + 1) * 512, [[0, 64], [1, 512]])
                P.dma("gpsimd", bdep, bcb[bsl][0], bc0src)
                t_bc = P.dma("gpsimd", bdep, bcb[bsl][1], bc1src)

                def ytfn(v, bsl=bsl):
                    v.tensor_tensor(out=ybuf[0:64, :], in0=Osb[0:64, 0:512],
                                    in1=bcb[bsl][0], op=OP.mult)
                    return v.tensor_tensor(out=ybuf[64:128, :], in0=Osb[0:64, 512:1024],
                                           in1=bcb[bsl][1], op=OP.mult)
                tok_yt = P.step("vector", [t_bc, tok_oev, tok_ytd_prev], ytfn)
                tok_bcb[bsl] = tok_yt
                pairp = 2 * g + j
                tok_ytd[pairp][qc] = P.dma("sync", [tok_yt],
                                           yt_d[128 * pairp:128 * (pairp + 1), 512 * qc:512 * (qc + 1)],
                                           ybuf)
                tok_ytd_prev = tok_ytd[pairp][qc]
        tok_S[("glast", g)] = tok_S[n_it - 1]

    pj = [psA[:, 0:1024], psA[:, 1024:2048]]
    tok_wp = [P.dma("gpsimd", [tok_mm2[NT - 1]], wp[:, C * cb:C * (cb + 1)],
                    wproj_d[128 * cb:128 * (cb + 1), :]) for cb in range(8)]
    tok_yq = [None] * NQC
    yTqb = [yTq, yTq2]
    tok_pj = [None] * NT
    tok_ev3 = [None] * NT
    tok_rdma = [None] * NT
    for it in range(NT):
        b = it % 2
        b3b = it % 3
        pr = it - 2
        pr3 = it - 3
        qc, ts = divmod(it, 4)

        def load_yq(qcl):
            dep = [tok_ytd[p][qcl] for p in range(8)]
            if qcl >= 2:
                dep.append(tok_pj[4 * (qcl - 2) + 3])
            tok_yq[qcl] = [P.dma("sync", dep, yTqb[qcl % 2][:, 512 * p:512 * (p + 1)],
                                 yt_d[128 * p:128 * (p + 1), 512 * qcl:512 * (qcl + 1)])
                           for p in range(8)]
        if ts == 0:
            if qc == 0:
                load_yq(0)
                load_yq(1)
            elif qc + 1 < NQC:
                load_yq(qc + 1)
        t_x3 = P.dma("sync", [tok_ev3[pr3], tok_mm2[NT - 1]] if pr3 >= 0 else [tok_mm2[NT - 1]],
                     xb3[b3b], x_d[128 * it:128 * (it + 1), :])

        def pjfn(t, b=b, qc=qc, ts=ts):
            for ch in range(2):
                for p in range(8):
                    ins = t.matmul(out=pj[b][:, 512 * ch:512 * (ch + 1)],
                                   lhsT=yTqb[qc % 2][:, 512 * p + 128 * ts:512 * p + 128 * (ts + 1)],
                                   rhs=wp[:, C * p + 512 * ch:C * p + 512 * (ch + 1)],
                                   start=(p == 0), stop=(p == 7))
            return ins
        depj = tok_yq[qc] + tok_wp + [tok_exp[n_it - 1] if it < 2 else tok_ev3[pr]]
        tok_pj[it] = P.step("tensor", depj, pjfn)
        tok_ev3[it] = P.step("vector", [tok_pj[it], t_x3] + ([tok_rdma[pr3]] if pr3 >= 0 else []),
                             lambda v, b=b, b3b=b3b: v.tensor_tensor(
                                 out=rs3[b3b], in0=pj[b], in1=xb3[b3b], op=OP.add))
        tok_rdma[it] = P.dma("sync" if it % 2 == 0 else "gpsimd", [tok_ev3[it]],
                             resid_d[128 * it:128 * (it + 1), :], rs3[b3b])

    ppg = [psA[:, 0:512], psA[:, 512:1024]]
    ppu = [psA[:, 1024:1536], psA[:, 1536:2048]]
    pdn = psB[:, 0:1024]
    pht = [psB[:, 1024:1280].bitcast(BF16), psB[:, 1536:1792].bitcast(BF16)]

    depA4 = [tok_wq_last, tok_pj[NT - 1], tok_rdma[NT - 1], tok_rdma[NT - 2], tok_ev3[NT - 1]]
    tok_wgT = P.dma_batch("sync", depA4, [(wgT[:, I * cb:I * (cb + 1)],
                          wgq_d[:, 128 * cb:128 * (cb + 1)]) for cb in range(8)], transpose=True)
    tok_wuT = P.dma_batch("sync", depA4, [(wuT[:, I * cb:I * (cb + 1)],
                          wuq_d[:, 128 * cb:128 * (cb + 1)]) for cb in range(8)], transpose=True)
    depB4 = [tok_wq_last, tok_yt, tok_xqdma[NT - 1], tok_ytd[7][NQC - 1]]
    tok_wdT = P.dma_batch("sync", depB4, [(wdT[:, C * ib:C * (ib + 1)],
                          wdq_d[:, 128 * ib:128 * (ib + 1)]) for ib in range(22)], transpose=True)
    w4loads = [tok_wgT, tok_wuT, tok_wdT]
    xqTb = [xqT, xqT2]

    tok_xqT = [None] * 8
    tg, tu, tsil, thm, tbm, tqs, ttq, tcp, tz = {}, {}, {}, {}, {}, {}, {}, {}, {}
    ttp, the, tdn = {}, {}, {}
    tok_out = [None] * NT
    tok_ydma = [None] * NT
    tok_r4 = [None] * NT
    tok_grp_consumed = [None] * 8

    def slot_of(cg):
        return divmod(cg, NIC) if cg >= 0 else None

    NCG = NT * NIC
    LAG, DLAG = 9, 10

    def emit_tphe(m_it, ic):
        w, io, bb = IC_W[ic], IC_OFF[ic], ic % 2
        nsub = w // 128
        m = (m_it, ic)
        cg = m_it * NIC + ic
        pm2 = slot_of(cg - 2)

        xj = cg % XD

        def tpfn(t, xj=xj, bb=bb, nsub=nsub):
            for ss in range(nsub):
                ins = t.transpose(out=pht[bb][:, 128 * ss:128 * (ss + 1)],
                                  in_=xqc[xj][:, 128 * ss:128 * (ss + 1)], identity=idn)
            return ins
        deptp = [tz[m], t_idn] + ([the[pm2]] if pm2 and pm2[0] >= 0 else [])
        ttp[m] = P.step("tensor", deptp, tpfn)
        the[m] = P.step("scalar", [ttp[m]] + ([tdn[pm2]] if pm2 and pm2[0] >= 0 else []),
                        lambda sc, bb=bb, w=w: sc.activation(
            out=hqT2[bb][:, 0:w], in_=pht[bb][:, 0:w], func=AF.Copy))

    def emit_dn(m_it, ic):
        w, io, bb = IC_W[ic], IC_OFF[ic], ic % 2
        nsub = w // 128
        m = (m_it, ic)

        def dnfn(t, ic=ic, io=io, bb=bb, nsub=nsub):
            for ss in range(nsub):
                for ch in range(2):
                    ins = t.matmul(out=pdn[:, 512 * ch:512 * (ch + 1)],
                                   lhsT=hqT2[bb][:, 128 * ss:128 * (ss + 1)],
                                   rhs=wdT[:, C * (io // 128 + ss) + 512 * ch:C * (io // 128 + ss) + 512 * (ch + 1)],
                                   start=(ic == 0 and ss == 0), stop=(ic == NIC - 1 and ss == nsub - 1))
            return ins
        depdn = [the[m]]
        if ic == 0:
            depdn += [tok_out[m_it - 1]] if m_it >= 1 else [tok_yt, tok_oev]
        tdn[m] = P.step("tensor", depdn, dnfn)
        if ic == NIC - 1:
            tok_out[m_it] = P.step("vector", [tdn[m], tok_r4[m_it]],
                                   lambda v, b=m_it % 3: v.tensor_tensor(
                out=rsb2[b], in0=pdn, in1=rsb2[b], op=OP.add))
            tok_ydma[m_it] = P.dma("sync", [tok_out[m_it]],
                                   y_d[128 * m_it:128 * (m_it + 1), :], rsb2[m_it % 3])

    for cg in range(NCG + DLAG + 1):
        if cg < NCG:
            it, ic = divmod(cg, NIC)
            grp, ti = divmod(it, 4)
            w, io, bb = IC_W[ic], IC_OFF[ic], ic % 2
            nb = w // 16
            n = (it, ic)
            p2 = slot_of(cg - 2)
            if ic == 0:
                if it == 0:
                    dep = [tok_xqdma[NT - 1], tok_yt, tok_ytd[7][NQC - 1]]
                    tok_xqT[0] = P.dma_batch("sync", dep, [
                        (xqTb[0][:, 512 * cb:512 * (cb + 1)], xq_d[0:512, 128 * cb:128 * (cb + 1)])
                        for cb in range(8)], transpose=True)
                if ti == 2 and grp + 1 < 8:
                    g2 = grp + 1
                    depp = [tok_xqdma[NT - 1]] + ([tok_grp_consumed[g2 - 2]] if g2 >= 2 else [])
                    tok_xqT[g2] = P.dma_batch("sync", depp, [
                        (xqTb[g2 % 2][:, 512 * cb:512 * (cb + 1)],
                         xq_d[512 * g2:512 * (g2 + 1), 128 * cb:128 * (cb + 1)])
                        for cb in range(8)], transpose=True)

        mtp = slot_of(cg - LAG)
        if mtp is not None and mtp[0] < NT:
            emit_tphe(*mtp)
        mdn = slot_of(cg - DLAG)
        if mdn is not None and mdn[0] < NT:
            emit_dn(*mdn)
        if cg < NCG and ic == 0:
            assert it < 3 or tok_ydma[it - 3] is not None
            tok_r4[it] = P.dma("sync", [tok_rdma[it]]
                               + ([tok_ydma[it - 3]] if it >= 3 else [tok_wq_last]),
                               rsb2[it % 3], resid_d[128 * it:128 * (it + 1), :])
        if cg < NCG:
            def gufn(t, ti=ti, io=io, w=w, bb=bb, gp=grp % 2, which=0):
                wT = wgT if which == 0 else wuT
                pp = ppg if which == 0 else ppu
                for cb in range(8):
                    ins = t.matmul(out=pp[bb][:, 0:w],
                                   lhsT=xqTb[gp][:, 512 * cb + 128 * ti:512 * cb + 128 * (ti + 1)],
                                   rhs=wT[:, I * cb + io:I * cb + io + w],
                                   start=(cb == 0), stop=(cb == 7))
                return ins
            depg = w4loads + [tok_xqT[grp]]
            if p2 is not None:
                depg = depg + [tsil[p2]]
            if it == 0 and ic == 0:
                depg = depg + [tok_ev3[NT - 1], tok_ev3[NT - 2], tok_pj[NT - 1],
                               tok_yt, tok_oev, tok_ytd[7][NQC - 1]]
            tg[n] = P.step("tensor", depg, lambda t, gufn=gufn: gufn(t, which=0))
            tu[n] = P.step("tensor", [thm[p2]] if p2 else [], lambda t, gufn=gufn: gufn(t, which=1))

        if cg < NCG:
            tsil[n] = P.step("scalar", [tg[n]] + ([thm[p2]] if p2 else []),
                             lambda sc, bb=bb, w=w: sc.activation(
                out=sg[bb][:, 0:w], in_=ppg[bb][:, 0:w], func=AF.Silu))
            thm[n] = P.step("vector", [tsil[n], tu[n]] + ([ttq[p2]] if p2 else []),
                            lambda v, bb=bb, w=w: v.tensor_tensor(
                out=hc[bb][:, 0:w], in0=sg[bb][:, 0:w], in1=ppu[bb][:, 0:w], op=OP.mult))
            tbm[n] = P.step("vector", [thm[n]] + ([tqs[p2]] if p2 else []),
                            lambda v, bb=bb, w=w, nb=nb: v.tensor_reduce(
                out=q4bm[bb][:, 0:nb], in_=r16(hc[bb][:, 0:w]), axis=AX.X, op=OP.max,
                apply_absolute_value=True))
            tqs[n] = P.step("scalar", [tbm[n]] + ([tz[p2]] if p2 else []),
                            lambda sc, bb=bb, nb=nb: (
                sc.activation(out=q4sc[bb][:, 0:nb], in_=q4bm[bb][:, 0:nb], func=AF.Relu, scale=1.0 / 6.0),
                sc.activation(out=q4s[bb][:, 0:nb], in_=q4sc[bb][:, 0:nb], func=AF.Copy, bias=1e-12))[-1])
            ttq[n] = P.step("vector", [tqs[n]] + ([tcp[p2]] if p2 else []),
                            lambda v, bb=bb, w=w, nb=nb: (
                v.reciprocal(out=q4r[bb][:, 0:nb], in_=q4s[bb][:, 0:nb]),
                v.drain(),
                v.tensor_tensor(out=r16(tc[bb][:, 0:w]), in0=r16(hc[bb][:, 0:w]),
                                in1=bc3(q4r[bb][:, 0:nb], nb, 16), op=OP.mult),
                v.tensor_scalar(out=u2c[bb][:, 0:w], in0=tc[bb][:, 0:w], scalar1=2.0, scalar2=-2.0,
                                op0=OP.min, op1=OP.max),
                v.tensor_scalar(out=u4c[bb][:, 0:w], in0=tc[bb][:, 0:w], scalar1=4.0, scalar2=-4.0,
                                op0=OP.min, op1=OP.max))[-1])
            depcp = [ttq[n]] + ([tz[p2]] if p2 else [])
            if cg >= XD:
                depcp.append(ttp[slot_of(cg - XD)])
            tcp[n] = P.step("scalar", depcp, lambda sc, bb=bb, w=w, xj=cg % XD: (
                sc.activation(out=Pc[bb][:, 0:w], in_=u2c[bb][:, 0:w], func=AF.Copy, bias=96.0),
                sc.activation(out=Qc[bb][:, 0:w], in_=u2c[bb][:, 0:w], func=AF.Copy, scale=-1.0, bias=-192.0),
                sc.activation(out=R1c[bb][:, 0:w], in_=u4c[bb][:, 0:w], func=AF.Copy, bias=192.0),
                sc.activation(out=R2c[bb][:, 0:w], in_=u4c[bb][:, 0:w], func=AF.Copy, scale=-1.0, bias=-416.0),
                sc.activation(out=xqc[xj][:, 0:w], in_=tc[bb][:, 0:w], func=AF.Copy, bias=320.0))[-1])
            tz[n] = P.step("vector", [tcp[n]], lambda v, bb=bb, w=w, xj=cg % XD, nb=nb: (
                v.tensor_tensor(out=Pc[bb][:, 0:w], in0=Pc[bb][:, 0:w], in1=Qc[bb][:, 0:w], op=OP.add),
                v.tensor_tensor(out=R1c[bb][:, 0:w], in0=R1c[bb][:, 0:w], in1=R2c[bb][:, 0:w], op=OP.add),
                v.tensor_tensor(out=R1c[bb][:, 0:w], in0=R1c[bb][:, 0:w], in1=xqc[xj][:, 0:w], op=OP.add),
                v.tensor_tensor(out=Pc[bb][:, 0:w], in0=Pc[bb][:, 0:w], in1=R1c[bb][:, 0:w], op=OP.add),
                v.tensor_tensor(out=r16(xqc[xj][:, 0:w]), in0=r16(Pc[bb][:, 0:w]),
                                in1=bc3(q4s[bb][:, 0:nb], nb, 16), op=OP.mult))[-1])
            if ic == NIC - 1:
                tok_grp_consumed[grp] = tu[n]

    with nc.Block() as block:
        P.emit(block)
    return nc


def _host_consts():
    f32 = np.float32
    pos = np.arange(T, dtype=f32)
    inv = (1.0 / (10000.0 ** (np.arange(0, HD, 2, dtype=f32) / f32(HD)))).astype(f32)
    fr = pos[:, None] * inv[None, :]
    cos2 = np.repeat(np.cos(fr).astype(f32), 2, axis=1).astype(f32)
    sin2 = np.repeat(np.sin(fr).astype(f32), 2, axis=1).astype(f32)
    sin2[:, 0::2] *= -1.0
    a = np.arange(128)
    tri = (a[None, :] < a[:, None]).astype(bfdt)
    sel = (np.eye(128) * -320.0).astype(bfdt)
    idn = np.eye(128).astype(bfdt)
    return cos2, sin2, tri, sel, idn


def kernel(L0_attn_x, L0_attn_w_attn, L0_attn_w_proj, L0_ffn_w_gate, L0_ffn_w_up, L0_ffn_w_down):
    x = np.ascontiguousarray(np.asarray(L0_attn_x, dtype=np.float32))
    w_attn = np.ascontiguousarray(np.asarray(L0_attn_w_attn, dtype=np.float32))
    w_proj = np.ascontiguousarray(np.asarray(L0_attn_w_proj, dtype=np.float32))
    w_gate = np.ascontiguousarray(np.asarray(L0_ffn_w_gate, dtype=np.float32))
    w_up = np.ascontiguousarray(np.asarray(L0_ffn_w_up, dtype=np.float32))
    w_down = np.ascontiguousarray(np.asarray(L0_ffn_w_down, dtype=np.float32))
    cos2, sin2, tri, sel, idn = _host_consts()
    nc = build_program()
    in_maps = [{
        "x": x[c], "w_attn": w_attn, "w_proj": w_proj,
        "w_gate": w_gate, "w_up": w_up, "w_down": w_down,
        "cos2": cos2, "sin2": sin2, "tri": tri, "sel": sel, "idn": idn,
    } for c in range(NCORES)]
    res = run_bass_kernel_spmd(nc, in_maps, core_ids=list(range(NCORES)))
    return np.stack([res.results[c]["y"] for c in range(NCORES)], axis=0).astype(np.float32)


# version 41
# speedup vs baseline: 1.0974x; 1.0101x over previous
import contextlib
import numpy as np
import ml_dtypes
import concourse.bass as bass
import concourse.mybir as mybir
from concourse.alu_op_type import AluOpType as OP
from concourse.bass_utils import run_bass_kernel_spmd

F32 = mybir.dt.float32
BF16 = mybir.dt.bfloat16
AF = mybir.ActivationFunctionType
AX = mybir.AxisListType
bfdt = ml_dtypes.bfloat16

T, C = 4096, 1024
NT, NQC = 32, 8
NH, NKV, HD = 16, 4, 64
QKV, KVD = 1536, 256
I = 2816
NIC = 6
IC_W = [512] * 5 + [256]
IC_OFF = [0, 512, 1024, 1536, 2048, 2560]
NCORES = 8
ENGINES = ["sync", "gpsimd", "tensor", "vector", "scalar"]


class Prog:
    def __init__(self, nc, es):
        self.nc = nc
        self.steps = []
        self.sems = {}
        self.cnt = {}
        for e in ENGINES + ["dsync", "dgp"]:
            self.sems[e] = es.enter_context(nc.semaphore("sem_" + e))
            self.cnt[e] = 0
        self.last_plain = {"sync": None, "gpsimd": None}
        self.last_tp = None

    def step(self, eng, deps, fn):
        self.cnt[eng] += 1
        tok = (eng, self.cnt[eng])
        self.steps.append((eng, [d for d in deps if d is not None], fn, "c", tok))
        return tok

    def dma(self, eng, deps, out, in_, transpose=False):
        deps = list(deps)
        if transpose:
            deps += [t for t in (self.last_plain["sync"], self.last_plain["gpsimd"]) if t]
        elif self.last_tp is not None:
            deps.append(self.last_tp)
        d = {"sync": "dsync", "gpsimd": "dgp"}[eng]
        self.cnt[d] += 16
        dval = self.cnt[d]

        def fn(eh, out=out, in_=in_, transpose=transpose):
            if transpose:
                return eh.dma_start_transpose(out=out, in_=in_)
            return eh.dma_start(out=out, in_=in_)
        self.steps.append((eng, [x for x in deps if x is not None], fn, "d", (d, dval)))
        self.cnt[eng] += 1
        tok = (eng, self.cnt[eng])
        self.steps.append((eng, [(d, dval)], None, "k", tok))
        if transpose:
            self.last_tp = tok
        else:
            self.last_plain[eng] = tok
        return tok

    def dma_batch(self, eng, deps, xfers, transpose=False):
        deps = list(deps)
        if transpose:
            deps += [t for t in (self.last_plain["sync"], self.last_plain["gpsimd"]) if t]
        elif self.last_tp is not None:
            deps.append(self.last_tp)
        d = {"sync": "dsync", "gpsimd": "dgp"}[eng]
        dval = None
        for j, (out, in_) in enumerate(xfers):
            self.cnt[d] += 16
            dval = self.cnt[d]

            def fn(eh, out=out, in_=in_, transpose=transpose):
                if transpose:
                    return eh.dma_start_transpose(out=out, in_=in_)
                return eh.dma_start(out=out, in_=in_)
            self.steps.append((eng, [x for x in (deps if j == 0 else []) if x is not None],
                               fn, "d", (d, dval)))
        self.cnt[eng] += 1
        tok = (eng, self.cnt[eng])
        self.steps.append((eng, [(d, dval)], None, "k", tok))
        if transpose:
            self.last_tp = tok
        else:
            self.last_plain[eng] = tok
        return tok

    def emit(self, block):
        for eng in ENGINES:
            mysteps = [s for s in self.steps if s[0] == eng]
            if not mysteps:
                continue

            def make(mysteps=mysteps, eng=eng):
                def stream(eh):
                    waited = {}
                    for _, deps, fn, kind, tok in mysteps:
                        for (sem, val) in deps:
                            if sem == eng:
                                continue
                            if waited.get(sem, -1) >= val:
                                continue
                            eh.wait_ge(self.sems[sem], val)
                            waited[sem] = val
                        if kind == "k":
                            eh.sem_inc(self.sems[eng], 1)
                        elif kind == "c":
                            fn(eh).then_inc(self.sems[eng], 1)
                        else:
                            fn(eh).then_inc(self.sems[tok[0]], 16)
                    if eng == "sync":
                        if self.cnt["dsync"]:
                            eh.wait_ge(self.sems["dsync"], self.cnt["dsync"])
                        if self.cnt["dgp"]:
                            eh.wait_ge(self.sems["dgp"], self.cnt["dgp"])
                        if self.cnt["vector"]:
                            eh.wait_ge(self.sems["vector"], self.cnt["vector"])
                return stream
            getattr(block, eng)(make())


def bc3(ap2d, nblk, rep):
    return ap2d.unsqueeze(2).broadcast_to([ap2d.shape[0], nblk, rep])


def r16(ap2d):
    return ap2d.rearrange("p (a s) -> p a s", s=16)


OFF_B = 90112
OFF_C = OFF_B + 53248
ARENA_BYTES = OFF_C + 61440


def build_program():
    nc = bass.Bass("TRN2", target_bir_lowering=False)

    x_d = nc.dram_tensor("x", [T, C], F32, kind="ExternalInput")
    wattn_d = nc.dram_tensor("w_attn", [C, QKV], F32, kind="ExternalInput")
    wproj_d = nc.dram_tensor("w_proj", [C, C], F32, kind="ExternalInput")
    wgate_d = nc.dram_tensor("w_gate", [I, C], F32, kind="ExternalInput")
    wup_d = nc.dram_tensor("w_up", [I, C], F32, kind="ExternalInput")
    wdown_d = nc.dram_tensor("w_down", [C, I], F32, kind="ExternalInput")
    cos_d = nc.dram_tensor("cos2", [T, HD], F32, kind="ExternalInput")
    sin_d = nc.dram_tensor("sin2", [T, HD], F32, kind="ExternalInput")
    tri_d = nc.dram_tensor("tri", [128, 128], BF16, kind="ExternalInput")
    sel_d = nc.dram_tensor("sel", [128, 128], BF16, kind="ExternalInput")
    idn_d = nc.dram_tensor("idn", [128, 128], BF16, kind="ExternalInput")
    y_d = nc.dram_tensor("y", [T, C], F32, kind="ExternalOutput")
    xn_d = nc.dram_tensor("xn_s", [T, C], BF16)
    xq_d = nc.dram_tensor("xq_s", [T, C], BF16)
    q_d = nc.dram_tensor("q_s", [T, C], BF16)
    k_d = nc.dram_tensor("k_s", [T, 2 * KVD], BF16)
    wgq_d = nc.dram_tensor("wgq_s", [I, C], BF16)
    wuq_d = nc.dram_tensor("wuq_s", [I, C], BF16)
    wdq_d = nc.dram_tensor("wdq_s", [C, I], BF16)
    yt_d = nc.dram_tensor("yt_s", [C, T], BF16)
    resid_d = nc.dram_tensor("resid_s", [T, C], F32)
    l_d = nc.dram_tensor("l_s", [4, 512], F32)
    linv_d = nc.dram_tensor("linv_s", [4, 512], F32)

    es = contextlib.ExitStack()
    P = Prog(nc, es)
    sb = lambda n, sh, dt: es.enter_context(nc.sbuf_tensor(n, sh, dt))[:]
    arena_h = es.enter_context(nc.sbuf_tensor("arena", [128, ARENA_BYTES // 2], BF16))

    def carve(off_b, nelem, dt, parts=128):
        assert off_b % 64 == 0
        nb = nelem * (4 if dt == F32 else 2)
        v = arena_h[0:parts, off_b // 2:(off_b + nb) // 2]
        if dt != BF16:
            v = v.bitcast(dt)
        return v

    psA = es.enter_context(nc.psum_tensor("psA", [128, 2048], F32))
    psB = es.enter_context(nc.psum_tensor("psB", [128, 2048], F32))

    tri = sb("tri_s", [128, 128], BF16)
    sel = sb("sel_s", [128, 128], BF16)
    idn = sb("idn_s", [128, 128], BF16)
    L128 = sb("L128_s", [128, 512], BF16)

    t_tri = P.dma("sync", [], tri, tri_d[:, :])
    t_sel = P.dma("sync", [], sel, sel_d[:, :])
    t_idn = P.dma("sync", [], idn, idn_d[:, :])
    t_L0 = P.step("gpsimd", [], lambda g: g.memset(L128, 0.0))

    xt = [carve(4096 * b, C, F32) for b in range(3)]
    xn = [carve(12288 + 4096 * b, C, F32) for b in range(4)]
    qt_ = [carve(28672 + 4096 * b, C, F32) for b in range(2)]
    qu_ = [carve(36864 + 4096 * b, C, F32) for b in range(2)]
    qU4 = [carve(45056 + 4096 * b, C, F32) for b in range(2)]
    qP = [carve(53248 + 2048 * b, C, BF16) for b in range(2)]
    qQ = [carve(57344 + 2048 * b, C, BF16) for b in range(2)]
    qR1 = [carve(61440 + 2048 * b, C, BF16) for b in range(2)]
    qR2 = [carve(65536 + 2048 * b, C, BF16) for b in range(2)]
    qXQ = [carve(69632 + 2048 * b, C, BF16) for b in range(2)]
    xnTg = [carve(0, 8 * 512, BF16), carve(8192, 8 * 512, BF16)]
    wa = carve(16384, 8 * QKV, BF16)
    v65 = carve(40960, NT * 260, BF16)
    wp = carve(0, 8 * C, BF16)
    xb3 = [carve(16384 + 4096 * b, C, F32) for b in range(3)]
    rs3 = [carve(28672 + 4096 * b, C, F32) for b in range(3)]
    wgT = carve(0, 8 * I, BF16)
    wuT = carve(45056, 8 * I, BF16)
    wtb2 = [carve(57600 + 4096 * b, C, F32) for b in range(3)]
    qt2 = [carve(69888 + 4096 * b, C, F32) for b in range(2)]
    qu2 = [carve(78080 + 4096 * b, C, F32) for b in range(2)]

    B = OFF_B
    cosb = carve(B + 0, NT * HD, F32)
    sinb = carve(B + 8192, NT * HD, F32)
    m1 = carve(B + 16384, C + KVD, F32)
    m2 = carve(B + 21504, C + KVD, F32)
    rot = [carve(B + 26624, C + KVD, F32), carve(B + 31744, C + KVD, F32)]
    sqq = m1
    qn = [carve(B + 36864, C, BF16), carve(B + 38912, C, BF16)]
    kn = [carve(B + 40960, KVD, BF16), carve(B + 41472, KVD, BF16)]
    qTg = [carve(B + 0, 2 * T, BF16), carve(B + 16384, 2 * T, BF16)]
    kTd = carve(B + 32768, T, BF16)
    PT = [carve(B + 40960, 1024, BF16), carve(B + 43008, 1024, BF16), carve(B + 45056, 1024, BF16)]
    ybuf = carve(B + 51200, 512, BF16)
    wdT = carve(B + 0, 22 * C, BF16)
    xqT = carve(B + 45056, 8 * 512, BF16)

    Cb = OFF_C
    yTq = carve(Cb + 0, 8 * 512, BF16)
    yTq2 = carve(Cb + 28672, 8 * 512, BF16)
    Osb = carve(Cb + 8192, 1024, F32, parts=65)
    L2 = [carve(Cb + 12288, 512, F32, parts=2), carve(Cb + 14336, 512, F32, parts=2)]
    Linv = [carve(Cb + 16384, 512, F32, parts=2), carve(Cb + 18432, 512, F32, parts=2)]
    bcb = [[carve(Cb + 20480, 512, F32, parts=64), carve(Cb + 22528, 512, F32, parts=64)],
           [carve(Cb + 24576, 512, F32, parts=64), carve(Cb + 26624, 512, F32, parts=64)]]
    hc = [carve(Cb + 2048 * b, 512, F32) for b in range(2)]
    tc = [carve(Cb + 4096 + 2048 * b, 512, F32) for b in range(2)]
    u2c = [carve(Cb + 8192 + 2048 * b, 512, F32) for b in range(2)]
    u4c = [carve(Cb + 12288 + 2048 * b, 512, F32) for b in range(2)]
    Pc = [carve(Cb + 16384 + 1024 * b, 512, BF16) for b in range(2)]
    Qc = [carve(Cb + 18432 + 1024 * b, 512, BF16) for b in range(2)]
    R1c = [carve(Cb + 20480 + 1024 * b, 512, BF16) for b in range(2)]
    R2c = [carve(Cb + 22528 + 1024 * b, 512, BF16) for b in range(2)]
    XD = 10
    xqc = [carve(Cb + 24576 + 1024 * j, 512, BF16) for j in range(XD)]
    sg = [carve(Cb + 34816 + 2048 * b, 512, F32) for b in range(2)]
    hqT2 = [carve(Cb + 38912 + 1024 * b, 512, BF16) for b in range(2)]
    rsb2 = [carve(Cb + 40960 + 4096 * b, C, F32) for b in range(3)]
    xqT2 = carve(Cb + 53248, 8 * 512, BF16)
    qU42 = [carve(Cb + 28672 + 4096 * b, C, F32) for b in range(2)]
    qP2 = [carve(Cb + 36864 + 2048 * b, C, BF16) for b in range(2)]
    qQ2 = [carve(Cb + 40960 + 2048 * b, C, BF16) for b in range(2)]
    qR12 = [carve(Cb + 45056 + 2048 * b, C, BF16) for b in range(2)]
    qR22 = [carve(Cb + 49152 + 2048 * b, C, BF16) for b in range(2)]
    qXQ2 = [carve(Cb + 53248 + 2048 * b, C, BF16) for b in range(2)]

    st1 = {nm: [sb(f"p1_{nm}{b}", [128, 1], F32) for b in range(2)]
           for nm in ["sum", "ssq", "negm", "msq", "nmse", "lnv", "rstd"]}
    qbm = [sb(f"qbm{b}", [128, 64], F32) for b in range(2)]
    qsc = [sb(f"qsc{b}", [128, 64], F32) for b in range(3)]
    qs_ = [sb(f"qs{b}", [128, 64], F32) for b in range(3)]
    qln = [sb(f"qln{b}", [128, 64], F32) for b in range(3)]
    qr_ = [sb(f"qr{b}", [128, 64], F32) for b in range(3)]
    junkB = carve(OFF_B + 0, C, F32)
    ms2 = [sb(f"p2ms{b}", [128, 20], F32) for b in range(2)]
    lnm = [sb(f"p2lm{b}", [128, 20], F32) for b in range(2)]
    rsd = [sb(f"p2rs{b}", [128, 20], F32) for b in range(2)]
    q4bm = [sb(f"q4bm{b}", [128, 32], F32) for b in range(2)]
    q4sc = [sb(f"q4sc{b}", [128, 32], F32) for b in range(2)]
    q4s = [sb(f"q4s{b}", [128, 32], F32) for b in range(2)]
    q4ln = [sb(f"q4ln{b}", [128, 32], F32) for b in range(2)]
    q4r = [sb(f"q4r{b}", [128, 32], F32) for b in range(2)]

    def quant(bufs, b, w, src, deps, dep_prev_z, dep_prev_out, vec_scale=False):
        nb = w // 16
        tb, ub = bufs["t"][b][:, 0:w], bufs["u"][b][:, 0:w]
        Pb, Qb = bufs["P"][b][:, 0:w], bufs["Q"][b][:, 0:w]
        xq = bufs["xq"][b][:, 0:w]
        bm, scb, sqb, lnb, rqb = (bufs[k][b][:, 0:nb] for k in ["bm", "sc", "s", "ln", "r"])
        pz = [dep_prev_z] if dep_prev_z else []
        po = [dep_prev_out] if dep_prev_out else []

        t_bm = P.step("vector", deps + pz, lambda v: v.tensor_reduce(
            out=bm, in_=r16(src), axis=AX.X, op=OP.max, apply_absolute_value=True))
        if vec_scale:
            def vsc(v):
                v.drain()
                v.tensor_scalar(out=sqb, in0=bm, scalar1=1.0 / 6.0, scalar2=1e-12, op0=OP.mult, op1=OP.max)
                v.drain()
                return v.reciprocal(out=rqb, in_=sqb)
            t_qs = P.step("vector", [t_bm], vsc)
        else:
            t_qs = P.step("scalar", [t_bm], lambda sc: (
                sc.activation(out=scb, in_=bm, func=AF.Relu, scale=1.0 / 6.0),
                sc.activation(out=sqb, in_=scb, func=AF.Copy, bias=1e-12),
                sc.activation(out=lnb, in_=sqb, func=AF.Ln),
                sc.activation(out=rqb, in_=lnb, func=AF.Exp, scale=-1.0))[-1])
        if "R1" in bufs:
            u4b = bufs["u4"][b][:, 0:w]
            R1, R2 = bufs["R1"][b][:, 0:w], bufs["R2"][b][:, 0:w]
            t_t = P.step("vector", [t_qs], lambda v: (
                v.tensor_tensor(out=r16(tb), in0=r16(src), in1=bc3(rqb, nb, 16), op=OP.mult),
                v.tensor_scalar(out=ub, in0=tb, scalar1=2.0, scalar2=-2.0, op0=OP.min, op1=OP.max),
                v.tensor_scalar(out=u4b, in0=tb, scalar1=4.0, scalar2=-4.0, op0=OP.min, op1=OP.max))[-1])
            t_cp = P.step("scalar", [t_t] + po, lambda sc: (
                sc.activation(out=Pb, in_=ub, func=AF.Copy, bias=96.0),
                sc.activation(out=Qb, in_=ub, func=AF.Copy, scale=-1.0, bias=-192.0),
                sc.activation(out=R1, in_=u4b, func=AF.Copy, bias=192.0),
                sc.activation(out=R2, in_=u4b, func=AF.Copy, scale=-1.0, bias=-416.0),
                sc.activation(out=xq, in_=tb, func=AF.Copy, bias=320.0))[-1])
            t_z = P.step("vector", [t_cp], lambda v: (
                v.tensor_tensor(out=Pb, in0=Pb, in1=Qb, op=OP.add),
                v.tensor_tensor(out=R1, in0=R1, in1=R2, op=OP.add),
                v.tensor_tensor(out=R1, in0=R1, in1=xq, op=OP.add),
                v.tensor_tensor(out=Pb, in0=Pb, in1=R1, op=OP.add),
                v.tensor_tensor(out=r16(xq), in0=r16(Pb), in1=bc3(sqb, nb, 16), op=OP.mult))[-1])
            return t_z
        Rb = xq
        t_t = P.step("vector", [t_qs], lambda v: (
            v.tensor_tensor(out=r16(tb), in0=r16(src), in1=bc3(rqb, nb, 16), op=OP.mult),
            v.tensor_scalar(out=ub, in0=tb, scalar1=2.0, scalar2=-2.0, op0=OP.min, op1=OP.max))[-1])
        t_ca = P.step("scalar", [t_t], lambda sc: (
            sc.activation(out=Pb, in_=ub, func=AF.Copy, bias=96.0),
            sc.activation(out=Qb, in_=ub, func=AF.Copy, scale=-1.0, bias=-192.0))[-1])
        t_v1 = P.step("vector", [t_ca], lambda v: (
            v.tensor_tensor(out=Pb, in0=Pb, in1=Qb, op=OP.add),
            v.tensor_scalar(out=ub, in0=tb, scalar1=4.0, scalar2=-4.0, op0=OP.min, op1=OP.max))[-1])
        t_cb = P.step("scalar", [t_v1], lambda sc: (
            sc.activation(out=Qb, in_=ub, func=AF.Copy, bias=192.0),
            sc.activation(out=Rb, in_=ub, func=AF.Copy, scale=-1.0, bias=-416.0))[-1])
        t_v2 = P.step("vector", [t_cb], lambda v: v.tensor_tensor(out=Qb, in0=Qb, in1=Rb, op=OP.add))
        t_cc = P.step("scalar", [t_v2] + po, lambda sc: sc.activation(out=Rb, in_=tb, func=AF.Copy, bias=320.0))
        t_z = P.step("vector", [t_cc], lambda v: (
            v.tensor_tensor(out=Qb, in0=Qb, in1=Rb, op=OP.add),
            v.tensor_tensor(out=Pb, in0=Pb, in1=Qb, op=OP.add),
            v.tensor_tensor(out=r16(xq), in0=r16(Pb), in1=bc3(sqb, nb, 16), op=OP.mult))[-1])
        return t_z

    qb1 = {"t": qt_, "u": qu_, "u4": qU4, "P": qP, "Q": qQ, "R1": qR1, "R2": qR2, "xq": qXQ,
           "bm": qbm, "sc": qsc, "s": qs_, "ln": qln, "r": qr_}

    NSTG = 11
    tokt = {}

    def p1_stage(i, k):
        b2, b3, b4 = i % 2, i % 3, i % 4
        g = lambda ii, kk: tokt.get((ii, kk)) if ii >= 0 else None
        if k == 0:
            return P.dma("sync", [g(i - 3, 3)], xt[b3], x_d[128 * i:128 * (i + 1), :])
        if k == 1:
            return P.step("vector", [tokt[(i, 0)], g(i - 2, 2)], lambda v: v.tensor_reduce(
                out=st1["sum"][b2], in_=xt[b3], axis=AX.X, op=OP.add))
        if k == 2:
            return P.step("scalar", [tokt[(i, 0)], tokt[(i, 1)], g(i - 2, 3)], lambda sc: (
                sc.activation(out=st1["negm"][b2], in_=st1["sum"][b2], func=AF.Copy, scale=-1.0 / C),
                sc.activation(out=st1["msq"][b2], in_=st1["negm"][b2], func=AF.Square),
                sc.activation(out=st1["nmse"][b2], in_=st1["msq"][b2], func=AF.Copy, scale=-1.0, bias=1e-5),
                sc.activation(out=junkB, in_=xt[b3], func=AF.Square, accum_out=st1["ssq"][b2]),
                sc.activation(out=st1["lnv"][b2], in_=st1["ssq"][b2], func=AF.Ln, scale=1.0 / C, bias=st1["nmse"][b2]),
                sc.activation(out=st1["rstd"][b2], in_=st1["lnv"][b2], func=AF.Exp, scale=-0.5))[-1])
        if k == 3:
            return P.step("vector", [tokt[(i, 2)], g(i - 4, 5), g(i - 4, 7)], lambda v: v.tensor_scalar(
                out=xn[b4], in0=xt[b3], scalar1=st1["negm"][b2], scalar2=st1["rstd"][b2],
                op0=OP.add, op1=OP.mult))
        if k == 4:
            return P.step("vector", [tokt[(i, 3)], g(i - 2, 6)], lambda v: v.tensor_reduce(
                out=qbm[b2], in_=r16(xn[b4]), axis=AX.X, op=OP.max, apply_absolute_value=True))
        if k == 5:
            return P.dma("gpsimd", [tokt[(i, 3)]], xn_d[128 * i:128 * (i + 1), :], xn[b4])
        if k == 6:
            return P.step("scalar", [tokt[(i, 4)], g(i - 3, 9)], lambda sc: (
                sc.activation(out=qsc[b3], in_=qbm[b2], func=AF.Relu, scale=1.0 / 6.0),
                sc.activation(out=qs_[b3], in_=qsc[b3], func=AF.Copy, bias=1e-12),
                sc.activation(out=qln[b3], in_=qs_[b3], func=AF.Ln),
                sc.activation(out=qr_[b3], in_=qln[b3], func=AF.Exp, scale=-1.0))[-1])
        if k == 7:
            return P.step("vector", [tokt[(i, 6)], g(i - 2, 8)], lambda v: (
                v.tensor_tensor(out=r16(qt_[b2]), in0=r16(xn[b4]), in1=bc3(qr_[b3], 64, 16), op=OP.mult),
                v.tensor_scalar(out=qu_[b2], in0=qt_[b2], scalar1=2.0, scalar2=-2.0, op0=OP.min, op1=OP.max),
                v.tensor_scalar(out=qU4[b2], in0=qt_[b2], scalar1=4.0, scalar2=-4.0, op0=OP.min, op1=OP.max))[-1])
        if k == 8:
            return P.step("scalar", [tokt[(i, 7)], g(i - 2, 9), g(i - 2, 10)], lambda sc: (
                sc.activation(out=qP[b2], in_=qu_[b2], func=AF.Copy, bias=96.0),
                sc.activation(out=qQ[b2], in_=qu_[b2], func=AF.Copy, scale=-1.0, bias=-192.0),
                sc.activation(out=qR1[b2], in_=qU4[b2], func=AF.Copy, bias=192.0),
                sc.activation(out=qR2[b2], in_=qU4[b2], func=AF.Copy, scale=-1.0, bias=-416.0),
                sc.activation(out=qXQ[b2], in_=qt_[b2], func=AF.Copy, bias=320.0))[-1])
        if k == 9:
            return P.step("vector", [tokt[(i, 8)]], lambda v: (
                v.tensor_tensor(out=qP[b2], in0=qP[b2], in1=qQ[b2], op=OP.add),
                v.tensor_tensor(out=qR1[b2], in0=qR1[b2], in1=qR2[b2], op=OP.add),
                v.tensor_tensor(out=qR1[b2], in0=qR1[b2], in1=qXQ[b2], op=OP.add),
                v.tensor_tensor(out=qP[b2], in0=qP[b2], in1=qR1[b2], op=OP.add),
                v.tensor_tensor(out=r16(qXQ[b2]), in0=r16(qP[b2]), in1=bc3(qs_[b3], 64, 16), op=OP.mult))[-1])
        if k == 10:
            return P.dma("sync", [tokt[(i, 9)]], xq_d[128 * i:128 * (i + 1), :], qXQ[b2])

    for tau in range(NT + NSTG - 1):
        for k in range(NSTG - 1, -1, -1):
            i = tau - k
            if 0 <= i < NT:
                tokt[(i, k)] = p1_stage(i, k)
    tok_xnv = [tokt[(i, 3)] for i in range(NT)]
    tok_xndma = [tokt[(i, 5)] for i in range(NT)]
    tok_xqdma = [tokt[(i, 10)] for i in range(NT)]
    tok_ln1 = [tokt[(i, 2)] for i in range(NT)]
    tok_z1 = [tokt[(i, 9)] for i in range(NT)]

    wtiles = ([("g", (i, 0), C) for i in range(22)] + [("u", (i, 0), C) for i in range(22)]
              + [("d", (i, cc), (1024 if cc < 2 else 768)) for i in range(8) for cc in range(3)])
    NWS = 7
    tokw = {}
    p1_last = tokt[(NT - 1, 10)]

    def p1b_stage(n, k):
        kind, (i, cc), w = wtiles[n]
        b2, b3 = n % 2, n % 3
        nb = w // 16
        srcd = {"g": wgate_d, "u": wup_d, "d": wdown_d}[kind]
        dstd = {"g": wgq_d, "u": wuq_d, "d": wdq_d}[kind]
        sl_src = srcd[128 * i:128 * (i + 1), 1024 * cc:1024 * cc + w]
        sl_dst = dstd[128 * i:128 * (i + 1), 1024 * cc:1024 * cc + w]

        def g(nn, kk):
            if nn >= 0:
                return tokw.get((nn, kk))
            return p1_last
        if k == 0:
            return P.dma("sync", [g(n - 3, 3)], wtb2[b3][:, 0:w], sl_src)
        if k == 1:
            return P.step("vector", [tokw[(n, 0)], g(n - 2, 2)], lambda v: v.tensor_reduce(
                out=qbm[b2][:, 0:nb], in_=r16(wtb2[b3][:, 0:w]), axis=AX.X, op=OP.max,
                apply_absolute_value=True))
        if k == 2:
            return P.step("scalar", [tokw[(n, 1)], g(n - 3, 5)], lambda sc: (
                sc.activation(out=qsc[b3][:, 0:nb], in_=qbm[b2][:, 0:nb], func=AF.Relu, scale=1.0 / 6.0),
                sc.activation(out=qs_[b3][:, 0:nb], in_=qsc[b3][:, 0:nb], func=AF.Copy, bias=1e-12),
                sc.activation(out=qln[b3][:, 0:nb], in_=qs_[b3][:, 0:nb], func=AF.Ln),
                sc.activation(out=qr_[b3][:, 0:nb], in_=qln[b3][:, 0:nb], func=AF.Exp, scale=-1.0))[-1])
        if k == 3:
            return P.step("vector", [tokw[(n, 2)], g(n - 2, 4)], lambda v: (
                v.tensor_tensor(out=r16(qt2[b2][:, 0:w]), in0=r16(wtb2[b3][:, 0:w]),
                                in1=bc3(qr_[b3][:, 0:nb], nb, 16), op=OP.mult),
                v.tensor_scalar(out=qu2[b2][:, 0:w], in0=qt2[b2][:, 0:w], scalar1=2.0, scalar2=-2.0,
                                op0=OP.min, op1=OP.max),
                v.tensor_scalar(out=qU42[b2][:, 0:w], in0=qt2[b2][:, 0:w], scalar1=4.0, scalar2=-4.0,
                                op0=OP.min, op1=OP.max))[-1])
        if k == 4:
            return P.step("scalar", [tokw[(n, 3)], g(n - 2, 5), g(n - 2, 6)], lambda sc: (
                sc.activation(out=qP2[b2][:, 0:w], in_=qu2[b2][:, 0:w], func=AF.Copy, bias=96.0),
                sc.activation(out=qQ2[b2][:, 0:w], in_=qu2[b2][:, 0:w], func=AF.Copy, scale=-1.0, bias=-192.0),
                sc.activation(out=qR12[b2][:, 0:w], in_=qU42[b2][:, 0:w], func=AF.Copy, bias=192.0),
                sc.activation(out=qR22[b2][:, 0:w], in_=qU42[b2][:, 0:w], func=AF.Copy, scale=-1.0, bias=-416.0),
                sc.activation(out=qXQ2[b2][:, 0:w], in_=qt2[b2][:, 0:w], func=AF.Copy, bias=320.0))[-1])
        if k == 5:
            return P.step("vector", [tokw[(n, 4)]], lambda v: (
                v.tensor_tensor(out=qP2[b2][:, 0:w], in0=qP2[b2][:, 0:w], in1=qQ2[b2][:, 0:w], op=OP.add),
                v.tensor_tensor(out=qR12[b2][:, 0:w], in0=qR12[b2][:, 0:w], in1=qR22[b2][:, 0:w], op=OP.add),
                v.tensor_tensor(out=qR12[b2][:, 0:w], in0=qR12[b2][:, 0:w], in1=qXQ2[b2][:, 0:w], op=OP.add),
                v.tensor_tensor(out=qP2[b2][:, 0:w], in0=qP2[b2][:, 0:w], in1=qR12[b2][:, 0:w], op=OP.add),
                v.tensor_tensor(out=r16(qXQ2[b2][:, 0:w]), in0=r16(qP2[b2][:, 0:w]),
                                in1=bc3(qs_[b3][:, 0:nb], nb, 16), op=OP.mult))[-1])
        if k == 6:
            return P.dma("gpsimd", [tokw[(n, 5)]], sl_dst, qXQ2[b2][:, 0:w])

    p1b_entries = []
    for tau in range(len(wtiles) + NWS - 1):
        for k in range(NWS - 1, -1, -1):
            n = tau - k
            if 0 <= n < len(wtiles):
                p1b_entries.append((n, k))

    def emit_p1b(cnt):
        while cnt > 0 and p1b_entries:
            n, k = p1b_entries.pop(0)
            tokw[(n, k)] = p1b_stage(n, k)
            cnt -= 1

    emit_p1b(len(p1b_entries))
    tok_wq_last = tokw[(len(wtiles) - 1, 6)]

    pq = [psA[:, 0:QKV], psB[:, 0:QKV]]
    tok_cos = P.dma("sync", [tok_ln1[NT - 1]], cosb.rearrange("p (n f) -> p n f", f=HD), cos_d[:, :].rearrange("(n p) f -> p n f", p=128))
    tok_sin = P.dma("sync", [tok_ln1[NT - 1]], sinb.rearrange("p (n f) -> p n f", f=HD), sin_d[:, :].rearrange("(n p) f -> p n f", p=128))
    tok_p1q = [tok_z1[NT - 1]]
    tok_wa = [P.dma("gpsimd", tok_p1q, wa[:, QKV * cb:QKV * (cb + 1)],
                    wattn_d[128 * cb:128 * (cb + 1), :]) for cb in range(8)]
    t_v65i = P.step("gpsimd", tok_p1q, lambda g: g.memset(v65, 1.0))

    tok_xnTg = [None] * 8
    tok_mm2 = [None] * NT
    tok_rope = [None] * NT
    tok_vcp = [None] * NT
    tok_qnv = [None] * NT
    tok_qdma = [None] * NT
    tok_kdma = [None] * NT
    tok_sq2 = [None] * NT
    tok_ms = [None] * NT
    tok_p1x = [tokt[(NT - 1, 3)], tokt[(28, 7)]]
    for i in range(NT):
        b = i % 2
        pr = i - 2
        grp, ti = divmod(i, 4)
        if ti == 0:
            dep = [tok_xndma[4 * grp + k] for k in range(4)] + tok_p1x
            if grp >= 2:
                dep.append(tok_mm2[4 * (grp - 2) + 3])
            tok_xnTg[grp] = [P.dma("sync", dep, xnTg[grp % 2][:, 512 * cb:512 * (cb + 1)],
                                   xn_d[512 * grp:512 * (grp + 1), 128 * cb:128 * (cb + 1)],
                                   transpose=True) for cb in range(8)]

        def mmfn(t, b=b, ti=ti, gb=grp % 2):
            for nch in range(3):
                for cb in range(8):
                    ins = t.matmul(out=pq[b][:, 512 * nch:512 * (nch + 1)],
                                   lhsT=xnTg[gb][:, 512 * cb + 128 * ti:512 * cb + 128 * (ti + 1)],
                                   rhs=wa[:, QKV * cb + 512 * nch:QKV * cb + 512 * (nch + 1)],
                                   start=(cb == 0), stop=(cb == 7))
            return ins
        depm = tok_xnTg[grp] + tok_wa
        if pr >= 0:
            depm = depm + [tok_rope[pr], tok_vcp[pr]]
        tok_mm2[i] = P.step("tensor", depm, mmfn)

        def ropefn(v, b=b, i=i):
            co = HD * i
            pst = list(pq[b].ap[0])
            pofs = pq[b].offset
            cview = cosb[:, co:co + HD]
            sview = sinb[:, co:co + HD]
            v.tensor_tensor(out=m1[:, 0:C].rearrange("p (h d) -> p h d", d=HD),
                            in0=pq[b][:, 0:C].rearrange("p (h d) -> p h d", d=HD),
                            in1=cview.unsqueeze(1).broadcast_to([128, NH, HD]), op=OP.mult)
            v.tensor_tensor(out=m2[:, 0:C].rearrange("p (h d) -> p h d", d=HD),
                            in0=bass.AP(pq[b].tensor, pofs + 1, [pst, [HD, NH], [2, HD // 2], [-1, 2]]),
                            in1=sview.unsqueeze(1).broadcast_to([128, NH, HD]), op=OP.mult)
            v.tensor_tensor(out=m1[:, C:C + KVD].rearrange("p (h d) -> p h d", d=HD),
                            in0=pq[b][:, C:C + KVD].rearrange("p (h d) -> p h d", d=HD),
                            in1=cview.unsqueeze(1).broadcast_to([128, NKV, HD]), op=OP.mult)
            v.tensor_tensor(out=m2[:, C:C + KVD].rearrange("p (h d) -> p h d", d=HD),
                            in0=bass.AP(pq[b].tensor, pofs + C + 1, [pst, [HD, NKV], [2, HD // 2], [-1, 2]]),
                            in1=sview.unsqueeze(1).broadcast_to([128, NKV, HD]), op=OP.mult)
            return v.tensor_tensor(out=rot[b], in0=m1, in1=m2, op=OP.add)
        rdeps = [tok_mm2[i], tok_cos, tok_sin]
        if i >= 1:
            rdeps.append(tok_sq2[i - 1])
        if pr >= 0:
            rdeps.append(tok_qnv[pr])
        tok_rope[i] = P.step("vector", rdeps, ropefn)

        def vcpfn(v, b=b, i=i):
            vdst = v65[:, 260 * i:260 * (i + 1)].rearrange("p (g u) -> p g u", u=65)[:, :, 0:64]
            return v.tensor_scalar(out=vdst, in0=pq[b][:, C + KVD:QKV].rearrange("p (g d) -> p g d", d=64),
                                   scalar1=1.0, scalar2=None, op0=OP.mult)
        tok_vcp[i] = P.step("vector", [tok_mm2[i], t_v65i], vcpfn)

        tok_sq2[i] = P.step("scalar", [tok_rope[i]] + ([tok_ms[i - 1]] if i >= 1 else []),
                            lambda sc, b=b: sc.activation(out=sqq, in_=rot[b], func=AF.Square))
        tok_ms[i] = P.step("vector", [tok_sq2[i]], lambda v, b=b: v.tensor_reduce(
            out=ms2[b], in_=sqq.rearrange("p (h d) -> p h d", d=HD), axis=AX.X, op=OP.add))
        t_rms = P.step("scalar", [tok_ms[i]], lambda sc, b=b: (
            sc.activation(out=lnm[b], in_=ms2[b], func=AF.Copy, scale=1.0 / HD, bias=1e-6),
            sc.activation(out=lnm[b], in_=lnm[b], func=AF.Ln),
            sc.activation(out=rsd[b], in_=lnm[b], func=AF.Exp, scale=-0.5))[-1])

        def qnfn(v, b=b):
            v.tensor_tensor(out=qn[b].rearrange("p (h d) -> p h d", d=HD),
                            in0=rot[b][:, 0:C].rearrange("p (h d) -> p h d", d=HD),
                            in1=bc3(rsd[b][:, 0:NH], NH, HD), op=OP.mult)
            return v.tensor_tensor(out=kn[b].rearrange("p (h d) -> p h d", d=HD),
                                   in0=rot[b][:, C:C + KVD].rearrange("p (h d) -> p h d", d=HD),
                                   in1=bc3(rsd[b][:, NH:NH + NKV], NKV, HD), op=OP.mult)
        tok_qnv[i] = P.step("vector", [t_rms] + ([tok_qdma[pr], tok_kdma[pr]] if pr >= 0 else []), qnfn)
        tok_qdma[i] = P.dma("sync", [tok_qnv[i]], q_d[128 * i:128 * (i + 1), :], qn[b])
        kd3 = k_d[128 * i:128 * (i + 1), :].rearrange("t (g dd) -> t g dd", dd=128)
        ks3 = kn[b].rearrange("t (g d) -> t g d", d=64)
        P.dma("sync", [tok_qnv[i]], kd3[:, :, 0:64], ks3)
        tok_kdma[i] = P.dma("sync", [tok_qnv[i]], kd3[:, :, 64:128], ks3)

    Spp = [psA[:, 0:1024], psA[:, 1024:2048], psB[:, 1024:2048]]
    O_ps = psB[:, 0:1024]

    n_it = 0
    tok_S = {}
    tok_exp = {}
    tok_avp = {}
    tok_oev = None
    tok_yt = None
    tok_hl_slot = [None, None]
    tok_ytd = [[None] * NQC for _ in range(8)]
    tok_ytd_prev = None
    tok_bcb = [None, None]
    pend_av = None
    lslot = 0

    for g in range(4):
        gb = g % 2
        depq = [tok_qdma[NT - 1]]
        if g >= 2:
            depq.append(tok_S[("glast", g - 2)])
        tq0 = P.dma("sync", depq, qTg[gb][:, 0:T], q_d[:, 256 * g:256 * g + 128], transpose=True)
        tq1 = P.dma("sync", depq, qTg[gb][:, T:2 * T], q_d[:, 256 * g + 128:256 * g + 256], transpose=True)
        depk = [tok_kdma[NT - 1]]
        if g >= 1:
            depk.append(tok_S[("glast", g - 1)])
        tk0 = P.dma("sync", depk, kTd, k_d[:, 128 * g:128 * (g + 1)], transpose=True)
        gload = [tq0, tq1, tk0]

        for j in range(2):
            pblk = T * j
            for qc in range(NQC):
                nkt = 4 * qc + 4
                for kt in range(nkt):
                    n = n_it
                    n_it += 1
                    sbi = n % 3
                    diag = kt >= 4 * qc
                    off = 128 * (kt - 4 * qc) if diag else 0

                    def Sfn(t, gb=gb, sbi=sbi, pblk=pblk, qc=qc, kt=kt, off=off, diag=diag):
                        t.matmul(out=Spp[sbi][:, off:512],
                                 lhsT=kTd[0:64, 128 * kt:128 * (kt + 1)],
                                 rhs=qTg[gb][0:64, pblk + 512 * qc + off:pblk + 512 * (qc + 1)],
                                 start=True, stop=not diag, tile_position=(0, 0))
                        ins = t.matmul(out=Spp[sbi][:, 512 + off:1024],
                                       lhsT=kTd[64:128, 128 * kt:128 * (kt + 1)],
                                       rhs=qTg[gb][64:128, pblk + 512 * qc + off:pblk + 512 * (qc + 1)],
                                       start=True, stop=not diag, tile_position=(64, 0))
                        if diag:
                            t.matmul(out=Spp[sbi][:, off:off + 128], lhsT=sel, rhs=tri,
                                     start=False, stop=True, skip_group_check=True)
                            ins = t.matmul(out=Spp[sbi][:, 512 + off:512 + off + 128], lhsT=sel, rhs=tri,
                                           start=False, stop=True, skip_group_check=True)
                        return ins
                    depS = gload + [t_sel, t_tri] + ([tok_exp[n - 3]] if n >= 3 else [])
                    if kt == 0 and tok_oev is not None:
                        depS = depS + [tok_oev]
                    tok_S[n] = P.step("tensor", depS, Sfn)

                    def expfn(sc, sbi=sbi, off=off):
                        pv = PT[sbi].rearrange("p (h w) -> p h w", h=2)[:, :, off:512]
                        sv = Spp[sbi].rearrange("p (h w) -> p h w", h=2)[:, :, off:512]
                        return sc.activation(out=pv, in_=sv, func=AF.Exp, scale=0.125)
                    depE = [tok_S[n]] + ([tok_avp[n - 3]] if n >= 3 else [])
                    tok_exp[n] = P.step("scalar", depE, expfn)

                    dep_av = [tok_exp[n], t_v65i]
                    if kt == 0 and tok_oev is not None:
                        dep_av = dep_av + [tok_oev]

                    def avfn(t, sbi=sbi, kt=kt, g=g, off=off, nkt=nkt):
                        for r in range(2):
                            ins = t.matmul(out=O_ps[0:65, 512 * r + off:512 * (r + 1)],
                                           lhsT=v65[:, 260 * kt + 65 * g:260 * kt + 65 * (g + 1)],
                                           rhs=PT[sbi][:, 512 * r + off:512 * (r + 1)],
                                           start=(kt == 0), stop=(kt == nkt - 1))
                        return ins
                    if pend_av is not None:
                        pn, pfn, pdep = pend_av
                        tok_avp[pn] = P.step("tensor", pdep, pfn)
                    pend_av = (n, avfn, dep_av)

                pn, pfn, pdep = pend_av
                tok_avp[pn] = P.step("tensor", pdep, pfn)
                pend_av = None

                tok_oev = P.step("vector", [tok_avp[n_it - 1]] + ([tok_yt] if tok_yt else []),
                                 lambda v: v.tensor_copy(out=Osb, in_=O_ps[0:65, :]))
                sl = lslot % 2
                lslot += 1
                lsrc = Osb[64:65, :].rearrange("p (a b) -> p a b", b=512)
                t_lo = P.dma("gpsimd", [tok_oev] + ([tok_hl_slot[sl]] if tok_hl_slot[sl] else []),
                             l_d[2 * sl:2 * (sl + 1), :], lsrc)
                t_li = P.dma("gpsimd", [t_lo], L2[sl], l_d[2 * sl:2 * (sl + 1), :])
                t_rc = P.step("vector", [t_li], lambda v, sl=sl: v.reciprocal(out=Linv[sl], in_=L2[sl]))
                tok_hl_slot[sl] = t_rc
                t_lo2 = P.dma("gpsimd", [t_rc], linv_d[2 * sl:2 * (sl + 1), :], Linv[sl])
                bsl = lslot % 2
                bdep = [t_lo2] + ([tok_bcb[bsl]] if tok_bcb[bsl] else [])
                bc0src = bass.AP(linv_d, 2 * sl * 512, [[0, 64], [1, 512]])
                bc1src = bass.AP(linv_d, (2 * sl + 1) * 512, [[0, 64], [1, 512]])
                P.dma("gpsimd", bdep, bcb[bsl][0], bc0src)
                t_bc = P.dma("gpsimd", bdep, bcb[bsl][1], bc1src)

                def ytfn(v, bsl=bsl):
                    v.tensor_tensor(out=ybuf[0:64, :], in0=Osb[0:64, 0:512],
                                    in1=bcb[bsl][0], op=OP.mult)
                    return v.tensor_tensor(out=ybuf[64:128, :], in0=Osb[0:64, 512:1024],
                                           in1=bcb[bsl][1], op=OP.mult)
                tok_yt = P.step("vector", [t_bc, tok_oev, tok_ytd_prev], ytfn)
                tok_bcb[bsl] = tok_yt
                pairp = 2 * g + j
                tok_ytd[pairp][qc] = P.dma("sync", [tok_yt],
                                           yt_d[128 * pairp:128 * (pairp + 1), 512 * qc:512 * (qc + 1)],
                                           ybuf)
                tok_ytd_prev = tok_ytd[pairp][qc]
        tok_S[("glast", g)] = tok_S[n_it - 1]

    pj = [psA[:, 0:1024], psA[:, 1024:2048]]
    tok_wp = [P.dma("gpsimd", [tok_mm2[NT - 1]], wp[:, C * cb:C * (cb + 1)],
                    wproj_d[128 * cb:128 * (cb + 1), :]) for cb in range(8)]
    tok_yq = [None] * NQC
    yTqb = [yTq, yTq2]
    tok_pj = [None] * NT
    tok_ev3 = [None] * NT
    tok_rdma = [None] * NT
    for it in range(NT):
        b = it % 2
        b3b = it % 3
        pr = it - 2
        pr3 = it - 3
        qc, ts = divmod(it, 4)

        def load_yq(qcl):
            dep = [tok_ytd[p][qcl] for p in range(8)]
            if qcl >= 2:
                dep.append(tok_pj[4 * (qcl - 2) + 3])
            tok_yq[qcl] = [P.dma("sync", dep, yTqb[qcl % 2][:, 512 * p:512 * (p + 1)],
                                 yt_d[128 * p:128 * (p + 1), 512 * qcl:512 * (qcl + 1)])
                           for p in range(8)]
        if ts == 0:
            if qc == 0:
                load_yq(0)
                load_yq(1)
            elif qc + 1 < NQC:
                load_yq(qc + 1)
        t_x3 = P.dma("gpsimd", [tok_ev3[pr3], tok_mm2[NT - 1]] if pr3 >= 0 else [tok_mm2[NT - 1]],
                     xb3[b3b], x_d[128 * it:128 * (it + 1), :])

        def pjfn(t, b=b, qc=qc, ts=ts):
            for ch in range(2):
                for p in range(8):
                    ins = t.matmul(out=pj[b][:, 512 * ch:512 * (ch + 1)],
                                   lhsT=yTqb[qc % 2][:, 512 * p + 128 * ts:512 * p + 128 * (ts + 1)],
                                   rhs=wp[:, C * p + 512 * ch:C * p + 512 * (ch + 1)],
                                   start=(p == 0), stop=(p == 7))
            return ins
        depj = tok_yq[qc] + tok_wp + [tok_exp[n_it - 1] if it < 2 else tok_ev3[pr]]
        tok_pj[it] = P.step("tensor", depj, pjfn)
        tok_ev3[it] = P.step("vector", [tok_pj[it], t_x3] + ([tok_rdma[pr3]] if pr3 >= 0 else []),
                             lambda v, b=b, b3b=b3b: v.tensor_tensor(
                                 out=rs3[b3b], in0=pj[b], in1=xb3[b3b], op=OP.add))
        tok_rdma[it] = P.dma("sync", [tok_ev3[it]],
                             resid_d[128 * it:128 * (it + 1), :], rs3[b3b])

    ppg = [psA[:, 0:512], psA[:, 512:1024]]
    ppu = [psA[:, 1024:1536], psA[:, 1536:2048]]
    pdn = psB[:, 0:1024]
    pht = [psB[:, 1024:1280].bitcast(BF16), psB[:, 1536:1792].bitcast(BF16)]

    depA4 = [tok_wq_last, tok_pj[NT - 1], tok_rdma[NT - 1], tok_rdma[NT - 2], tok_ev3[NT - 1]]
    tok_wgT = P.dma_batch("sync", depA4, [(wgT[:, I * cb:I * (cb + 1)],
                          wgq_d[:, 128 * cb:128 * (cb + 1)]) for cb in range(8)], transpose=True)
    tok_wuT = P.dma_batch("sync", depA4, [(wuT[:, I * cb:I * (cb + 1)],
                          wuq_d[:, 128 * cb:128 * (cb + 1)]) for cb in range(8)], transpose=True)
    depB4 = [tok_wq_last, tok_yt, tok_xqdma[NT - 1], tok_ytd[7][NQC - 1]]
    tok_wdT = P.dma_batch("sync", depB4, [(wdT[:, C * ib:C * (ib + 1)],
                          wdq_d[:, 128 * ib:128 * (ib + 1)]) for ib in range(22)], transpose=True)
    w4loads = [tok_wgT, tok_wuT, tok_wdT]
    xqTb = [xqT, xqT2]

    tok_xqT = [None] * 8
    tg, tu, tsil, thm, tbm, tqs, ttq, tcp, tz = {}, {}, {}, {}, {}, {}, {}, {}, {}
    ttp, the, tdn = {}, {}, {}
    tok_out = [None] * NT
    tok_ydma = [None] * NT
    tok_r4 = [None] * NT
    tok_grp_consumed = [None] * 8

    def slot_of(cg):
        return divmod(cg, NIC) if cg >= 0 else None

    NCG = NT * NIC
    LAG, DLAG = 9, 10

    def emit_tphe(m_it, ic):
        w, io, bb = IC_W[ic], IC_OFF[ic], ic % 2
        nsub = w // 128
        m = (m_it, ic)
        cg = m_it * NIC + ic
        pm2 = slot_of(cg - 2)

        xj = cg % XD

        def tpfn(t, xj=xj, bb=bb, nsub=nsub):
            for ss in range(nsub):
                ins = t.transpose(out=pht[bb][:, 128 * ss:128 * (ss + 1)],
                                  in_=xqc[xj][:, 128 * ss:128 * (ss + 1)], identity=idn)
            return ins
        deptp = [tz[m], t_idn] + ([the[pm2]] if pm2 and pm2[0] >= 0 else [])
        ttp[m] = P.step("tensor", deptp, tpfn)
        the[m] = P.step("scalar", [ttp[m]] + ([tdn[pm2]] if pm2 and pm2[0] >= 0 else []),
                        lambda sc, bb=bb, w=w: sc.activation(
            out=hqT2[bb][:, 0:w], in_=pht[bb][:, 0:w], func=AF.Copy))

    def emit_dn(m_it, ic):
        w, io, bb = IC_W[ic], IC_OFF[ic], ic % 2
        nsub = w // 128
        m = (m_it, ic)

        def dnfn(t, ic=ic, io=io, bb=bb, nsub=nsub):
            for ss in range(nsub):
                for ch in range(2):
                    ins = t.matmul(out=pdn[:, 512 * ch:512 * (ch + 1)],
                                   lhsT=hqT2[bb][:, 128 * ss:128 * (ss + 1)],
                                   rhs=wdT[:, C * (io // 128 + ss) + 512 * ch:C * (io // 128 + ss) + 512 * (ch + 1)],
                                   start=(ic == 0 and ss == 0), stop=(ic == NIC - 1 and ss == nsub - 1))
            return ins
        depdn = [the[m]]
        if ic == 0:
            depdn += [tok_out[m_it - 1]] if m_it >= 1 else [tok_yt, tok_oev]
        tdn[m] = P.step("tensor", depdn, dnfn)
        if ic == NIC - 1:
            tok_out[m_it] = P.step("vector", [tdn[m], tok_r4[m_it]],
                                   lambda v, b=m_it % 3: v.tensor_tensor(
                out=rsb2[b], in0=pdn, in1=rsb2[b], op=OP.add))
            tok_ydma[m_it] = P.dma("sync", [tok_out[m_it]],
                                   y_d[128 * m_it:128 * (m_it + 1), :], rsb2[m_it % 3])

    for cg in range(NCG + DLAG + 1):
        if cg < NCG:
            it, ic = divmod(cg, NIC)
            grp, ti = divmod(it, 4)
            w, io, bb = IC_W[ic], IC_OFF[ic], ic % 2
            nb = w // 16
            n = (it, ic)
            p2 = slot_of(cg - 2)
            if ic == 0:
                if it == 0:
                    dep = [tok_xqdma[NT - 1], tok_yt, tok_ytd[7][NQC - 1]]
                    tok_xqT[0] = P.dma_batch("sync", dep, [
                        (xqTb[0][:, 512 * cb:512 * (cb + 1)], xq_d[0:512, 128 * cb:128 * (cb + 1)])
                        for cb in range(8)], transpose=True)
                if ti == 2 and grp + 1 < 8:
                    g2 = grp + 1
                    depp = [tok_xqdma[NT - 1]] + ([tok_grp_consumed[g2 - 2]] if g2 >= 2 else [])
                    tok_xqT[g2] = P.dma_batch("sync", depp, [
                        (xqTb[g2 % 2][:, 512 * cb:512 * (cb + 1)],
                         xq_d[512 * g2:512 * (g2 + 1), 128 * cb:128 * (cb + 1)])
                        for cb in range(8)], transpose=True)

        mtp = slot_of(cg - LAG)
        if mtp is not None and mtp[0] < NT:
            emit_tphe(*mtp)
        mdn = slot_of(cg - DLAG)
        if mdn is not None and mdn[0] < NT:
            emit_dn(*mdn)
        if cg < NCG and ic == 0:
            assert it < 3 or tok_ydma[it - 3] is not None
            tok_r4[it] = P.dma("sync", [tok_rdma[it]]
                               + ([tok_ydma[it - 3]] if it >= 3 else [tok_wq_last]),
                               rsb2[it % 3], resid_d[128 * it:128 * (it + 1), :])
        if cg < NCG:
            def gufn(t, ti=ti, io=io, w=w, bb=bb, gp=grp % 2, which=0):
                wT = wgT if which == 0 else wuT
                pp = ppg if which == 0 else ppu
                for cb in range(8):
                    ins = t.matmul(out=pp[bb][:, 0:w],
                                   lhsT=xqTb[gp][:, 512 * cb + 128 * ti:512 * cb + 128 * (ti + 1)],
                                   rhs=wT[:, I * cb + io:I * cb + io + w],
                                   start=(cb == 0), stop=(cb == 7))
                return ins
            depg = w4loads + [tok_xqT[grp]]
            if p2 is not None:
                depg = depg + [tsil[p2]]
            if it == 0 and ic == 0:
                depg = depg + [tok_ev3[NT - 1], tok_ev3[NT - 2], tok_pj[NT - 1],
                               tok_yt, tok_oev, tok_ytd[7][NQC - 1]]
            tg[n] = P.step("tensor", depg, lambda t, gufn=gufn: gufn(t, which=0))
            tu[n] = P.step("tensor", [thm[p2]] if p2 else [], lambda t, gufn=gufn: gufn(t, which=1))

        if cg < NCG:
            tsil[n] = P.step("scalar", [tg[n]] + ([thm[p2]] if p2 else []),
                             lambda sc, bb=bb, w=w: sc.activation(
                out=sg[bb][:, 0:w], in_=ppg[bb][:, 0:w], func=AF.Silu))
            thm[n] = P.step("vector", [tsil[n], tu[n]] + ([ttq[p2]] if p2 else []),
                            lambda v, bb=bb, w=w: v.tensor_tensor(
                out=hc[bb][:, 0:w], in0=sg[bb][:, 0:w], in1=ppu[bb][:, 0:w], op=OP.mult))
            tbm[n] = P.step("vector", [thm[n]] + ([tqs[p2]] if p2 else []),
                            lambda v, bb=bb, w=w, nb=nb: v.tensor_reduce(
                out=q4bm[bb][:, 0:nb], in_=r16(hc[bb][:, 0:w]), axis=AX.X, op=OP.max,
                apply_absolute_value=True))
            tqs[n] = P.step("scalar", [tbm[n]] + ([tz[p2]] if p2 else []),
                            lambda sc, bb=bb, nb=nb: (
                sc.activation(out=q4sc[bb][:, 0:nb], in_=q4bm[bb][:, 0:nb], func=AF.Relu, scale=1.0 / 6.0),
                sc.activation(out=q4s[bb][:, 0:nb], in_=q4sc[bb][:, 0:nb], func=AF.Copy, bias=1e-12))[-1])
            ttq[n] = P.step("vector", [tqs[n]] + ([tcp[p2]] if p2 else []),
                            lambda v, bb=bb, w=w, nb=nb: (
                v.reciprocal(out=q4r[bb][:, 0:nb], in_=q4s[bb][:, 0:nb]),
                v.drain(),
                v.tensor_tensor(out=r16(tc[bb][:, 0:w]), in0=r16(hc[bb][:, 0:w]),
                                in1=bc3(q4r[bb][:, 0:nb], nb, 16), op=OP.mult),
                v.tensor_scalar(out=u2c[bb][:, 0:w], in0=tc[bb][:, 0:w], scalar1=2.0, scalar2=-2.0,
                                op0=OP.min, op1=OP.max),
                v.tensor_scalar(out=u4c[bb][:, 0:w], in0=tc[bb][:, 0:w], scalar1=4.0, scalar2=-4.0,
                                op0=OP.min, op1=OP.max))[-1])
            depcp = [ttq[n]] + ([tz[p2]] if p2 else [])
            if cg >= XD:
                depcp.append(ttp[slot_of(cg - XD)])
            tcp[n] = P.step("scalar", depcp, lambda sc, bb=bb, w=w, xj=cg % XD: (
                sc.activation(out=Pc[bb][:, 0:w], in_=u2c[bb][:, 0:w], func=AF.Copy, bias=96.0),
                sc.activation(out=Qc[bb][:, 0:w], in_=u2c[bb][:, 0:w], func=AF.Copy, scale=-1.0, bias=-192.0),
                sc.activation(out=R1c[bb][:, 0:w], in_=u4c[bb][:, 0:w], func=AF.Copy, bias=192.0),
                sc.activation(out=R2c[bb][:, 0:w], in_=u4c[bb][:, 0:w], func=AF.Copy, scale=-1.0, bias=-416.0),
                sc.activation(out=xqc[xj][:, 0:w], in_=tc[bb][:, 0:w], func=AF.Copy, bias=320.0))[-1])
            tz[n] = P.step("vector", [tcp[n]], lambda v, bb=bb, w=w, xj=cg % XD, nb=nb: (
                v.tensor_tensor(out=Pc[bb][:, 0:w], in0=Pc[bb][:, 0:w], in1=Qc[bb][:, 0:w], op=OP.add),
                v.tensor_tensor(out=R1c[bb][:, 0:w], in0=R1c[bb][:, 0:w], in1=R2c[bb][:, 0:w], op=OP.add),
                v.tensor_tensor(out=R1c[bb][:, 0:w], in0=R1c[bb][:, 0:w], in1=xqc[xj][:, 0:w], op=OP.add),
                v.tensor_tensor(out=Pc[bb][:, 0:w], in0=Pc[bb][:, 0:w], in1=R1c[bb][:, 0:w], op=OP.add),
                v.tensor_tensor(out=r16(xqc[xj][:, 0:w]), in0=r16(Pc[bb][:, 0:w]),
                                in1=bc3(q4s[bb][:, 0:nb], nb, 16), op=OP.mult))[-1])
            if ic == NIC - 1:
                tok_grp_consumed[grp] = tu[n]

    with nc.Block() as block:
        P.emit(block)
    return nc


def _host_consts():
    f32 = np.float32
    pos = np.arange(T, dtype=f32)
    inv = (1.0 / (10000.0 ** (np.arange(0, HD, 2, dtype=f32) / f32(HD)))).astype(f32)
    fr = pos[:, None] * inv[None, :]
    cos2 = np.repeat(np.cos(fr).astype(f32), 2, axis=1).astype(f32)
    sin2 = np.repeat(np.sin(fr).astype(f32), 2, axis=1).astype(f32)
    sin2[:, 0::2] *= -1.0
    a = np.arange(128)
    tri = (a[None, :] < a[:, None]).astype(bfdt)
    sel = (np.eye(128) * -320.0).astype(bfdt)
    idn = np.eye(128).astype(bfdt)
    return cos2, sin2, tri, sel, idn


def kernel(L0_attn_x, L0_attn_w_attn, L0_attn_w_proj, L0_ffn_w_gate, L0_ffn_w_up, L0_ffn_w_down):
    x = np.ascontiguousarray(np.asarray(L0_attn_x, dtype=np.float32))
    w_attn = np.ascontiguousarray(np.asarray(L0_attn_w_attn, dtype=np.float32))
    w_proj = np.ascontiguousarray(np.asarray(L0_attn_w_proj, dtype=np.float32))
    w_gate = np.ascontiguousarray(np.asarray(L0_ffn_w_gate, dtype=np.float32))
    w_up = np.ascontiguousarray(np.asarray(L0_ffn_w_up, dtype=np.float32))
    w_down = np.ascontiguousarray(np.asarray(L0_ffn_w_down, dtype=np.float32))
    cos2, sin2, tri, sel, idn = _host_consts()
    nc = build_program()
    in_maps = [{
        "x": x[c], "w_attn": w_attn, "w_proj": w_proj,
        "w_gate": w_gate, "w_up": w_up, "w_down": w_down,
        "cos2": cos2, "sin2": sin2, "tri": tri, "sel": sel, "idn": idn,
    } for c in range(NCORES)]
    res = run_bass_kernel_spmd(nc, in_maps, core_ids=list(range(NCORES)))
    return np.stack([res.results[c]["y"] for c in range(NCORES)], axis=0).astype(np.float32)
